# version 26
# baseline (speedup 1.0000x reference)
import contextlib
import numpy as np
import ml_dtypes
import concourse.bass as bass
import concourse.mybir as mybir
from concourse.bass_utils import run_bass_kernel_spmd

F32 = mybir.dt.float32
BF16 = mybir.dt.bfloat16
AF = mybir.ActivationFunctionType
ALU = mybir.AluOpType
AX = mybir.AxisListType

D = 1024
H = 16
KEXP = 8
RR = 64
KR = KEXP * RR
NCORES = 8
ENGINES = ("tensor", "scalar", "vector", "sync", "gpsimd")



def _nm(x):
    t = getattr(x, "tensor", None)
    return t.name if t is not None else x.name

class Buf:
    __slots__ = ("w", "readers")
    def __init__(self):
        self.w = None
        self.readers = []


class Builder:
    def __init__(self, nc, sems):
        self.nc = nc
        self.sems = sems
        self.ops = {e: [] for e in ENGINES}
        self.memo = {e: {} for e in ENGINES}

    def wait_tok(self, eng, tok, force=False):
        if tok is None:
            return
        s, v, prod = tok
        if not force:
            if prod == eng and s in ("tp", "gp"):
                return
            if self.memo[eng].get(s, -1) >= v:
                return
        if self.memo[eng].get(s, -1) < v:
            self.memo[eng][s] = v
        h = self.sems[s][0]
        self.ops[eng].append(lambda e, h=h, v=v: e.wait_ge(h, v))

    def emit(self, eng, fn, sem=None, n=1):
        if sem is None:
            self.ops[eng].append(fn)
            return None
        rec = self.sems[sem]
        rec[1] += n
        h = rec[0]
        self.ops[eng].append(lambda e, fn=fn, h=h, n=n: fn(e).then_inc(h, n))
        return (sem, rec[1], eng)

    def group(self, eng, reads, writes, emits, sem, n=1):
        for b in reads:
            if b is not None:
                self.wait_tok(eng, b.w)
        for b in writes:
            if b is None:
                continue
            self.wait_tok(eng, b.w)
            for t in b.readers:
                self.wait_tok(eng, t)
        tok = None
        if sem not in ("tp", "ac", "vd", "gp"):
            for fn in emits:
                tok = self.emit(eng, fn, sem=sem, n=16)
        elif sem in ("ac", "vd", "gp") and len(emits) > 1:
            h = self.sems[sem][0]
            for i, fn in enumerate(emits):
                if i > 0:
                    v = self.sems[sem][1]
                    self.ops[eng].append(lambda e, h=h, v=v: e.wait_ge(h, v))
                tok = self.emit(eng, fn, sem=sem, n=1)
        else:
            for i, fn in enumerate(emits):
                tok = self.emit(eng, fn, sem=sem if i == len(emits) - 1 else None, n=n)
        for b in reads:
            if b is not None:
                b.readers.append(tok)
        for b in writes:
            if b is not None:
                b.w = tok
                b.readers = []
        return tok

    def cur_toks(self):
        toks = [("tp", self.sems["tp"][1], "tensor"),
                ("ac", self.sems["ac"][1], "scalar"),
                ("vd", self.sems["vd"][1], "vector")]
        for s, rec in self.sems.items():
            if s not in ("tp", "ac", "vd") and rec[1] > 0:
                toks.append((s, rec[1], "sync"))
        return toks

    def fence(self, engines=ENGINES, force=False):
        toks = self.cur_toks()
        for e in engines:
            for t in toks:
                self.wait_tok(e, t, force=force)


DBG = False

def build_program(T=2048, SEQ=2):
    nc = bass.Bass("TRN2", target_bir_lowering=False)
    TC = T // 512
    NI = T // 128
    NICH = T // 512

    def di(name, shape, dt=F32):
        return nc.dram_tensor(name, list(shape), dt, kind="ExternalInput").ap()

    x_in = di("x", [SEQ, T, D])
    qkv_w = di("qkv_w", [3 * D, D]); out_w = di("out_w", [D, D])
    hex_w = di("hex_w", [6, D]); lam_logit = di("lam_logit", [1])
    q6_w = di("q6_w", [6, D]); transforms = di("transforms", [7, 6, 6])
    tw_in = di("tw", [H, 7]); scale_logit = di("scale_logit", [1])
    sips_alpha = di("sips_alpha", [1]); sips_beta = di("sips_beta", [1])
    w_q = di("w_q", [D, D]); w_k = di("w_k", [D, D]); w_o = di("w_o", [D, D])
    w_inv = di("w_inv", [D, D])
    v_fwd = di("v_fwd", [KEXP, D, RR]); w_fwd = di("w_fwd", [KEXP, D, RR])
    u_fwd = di("u_fwd", [KEXP, D, RR]); b_fwd = di("b_fwd", [KEXP, D])
    v_inv = di("v_inv", [KEXP, D, RR]); w_inv_exp = di("w_inv_exp", [KEXP, D, RR])
    u_inv = di("u_inv", [KEXP, D, RR]); b_inv = di("b_inv", [KEXP, D])
    rw1 = di("rw1", [D, D]); rb1 = di("rb1", [D]); rw2 = di("rw2", [KEXP, D])
    rb2 = di("rb2", [KEXP]); alpha_bi = di("alpha_bi", [1]); expert_bias = di("expert_bias", [KEXP])
    c_eye = di("c_eye", [128, 128], BF16)
    c_eyef = di("c_eyef", [128, 128], F32)
    c_negI = di("c_negI", [128, 128], BF16)
    c_maskW = di("c_maskW", [128, 896], BF16)
    c_eye64 = di("c_eye64", [64, 64], BF16)
    c_eyeT6 = di("c_eyeT6", [6, 12], BF16)
    c_eyeB6 = di("c_eyeB6", [6, 12], BF16)
    c_selE = di("c_selE", [8, 512], BF16)
    c_TTeye = di("c_TTeye", [12, 12 * H], BF16)
    c_ones1 = di("c_ones1", [1, 128], F32)
    c_invn = di("c_invn", [128, NI], F32)

    y_out = nc.dram_tensor("y", [SEQ, T, D], F32, kind="ExternalOutput").ap()
    dbg_o = {}
    if DBG:
        for nm_, sh in (("d_xn", [128, 8 * T]), ("d_q", [128, 8 * T]), ("d_k", [128, 8 * T]),
                        ("d_att", [128, 8 * T]), ("d_ffn", [128, 8 * T]), ("d_vext", [128, (T // 128) * 16 * 65]),
                        ("d_lhs0", [12, T]), ("d_lhsL", [12, T]), ("d_crh0", [12, T]), ("d_lsl", [64, 512])):
            dbg_o[nm_] = nc.dram_tensor(nm_, sh, BF16 if nm_ != "d_lsl" else F32, kind="ExternalOutput").ap()
        dbg_o["d_stats"] = nc.dram_tensor("d_stats", [128, 2 * (T // 128)], F32, kind="ExternalOutput").ap()
        dbg_o["d_xt"] = nc.dram_tensor("d_xt", [128, 1024], BF16, kind="ExternalOutput").ap()
        dbg_o["d_var"] = nc.dram_tensor("d_var", [128, 2 * (T // 128)], F32, kind="ExternalOutput").ap()
        dbg_o["d_xb0"] = nc.dram_tensor("d_xb0", [128, 1024], F32, kind="ExternalOutput").ap()
        dbg_o["d_xsq"] = nc.dram_tensor("d_xsq", [128, 1024], F32, kind="ExternalOutput").ap()
        dbg_o["d_ptb"] = nc.dram_tensor("d_ptb", [128, 1024], F32, kind="ExternalOutput").ap()

    dr = {}
    def dram(name, shape):
        dr[name] = nc.dram_tensor(name, list(shape), BF16, kind="Internal").ap()
        return dr[name]
    wbf_qkv = dram("wbf_qkv", [3 * D, D])
    wbf_outw = dram("wbf_outw", [D, D]); wbf_wq = dram("wbf_wq", [D, D])
    wbf_wk = dram("wbf_wk", [D, D]); wbf_wo = dram("wbf_wo", [D, D])
    wbf_winv = dram("wbf_winv", [D, D]); wbf_rw1 = dram("wbf_rw1", [D, D])
    vwu_src = {"vbf_f": v_fwd, "wbfr_f": w_fwd, "ubf_f": u_fwd,
               "vbf_i": w_inv_exp, "wbfr_i": v_inv, "ubf_i": u_inv}
    for nm in vwu_src:
        dram(nm, [D, KR])
    mh_dram = dram("mh_dram", [16, 36])

    stk = contextlib.ExitStack()
    with stk:
        E = stk.enter_context
        sems = {s: [E(nc.semaphore(s)), 0] for s in ("ds", "dg", "tp", "ac", "vd", "sxb0", "sxb1", "sxb2", "sxb3", "sqT", "sqT1", "svT", "swp0", "swp1", "svS", "swS", "suS", "sow", "swo", "stl0", "stl1", "sst0", "sst1", "gp", "dgq")}
        B = Builder(nc, sems)
        bufs = {}
        def bf(key):
            if key not in bufs:
                bufs[key] = Buf()
            return bufs[key]

        def sb(st_, name, shape, dt=F32, side="left"):
            return st_.enter_context(nc.sbuf_tensor(name, list(shape), dt, side=side))

        eye = sb(stk, "eye", [128, 128], BF16)
        eyef = sb(stk, "eyef", [128, 128], F32)
        negI = sb(stk, "negI", [128, 128], BF16)
        maskW = sb(stk, "maskW", [128, 896], BF16)
        eye64 = sb(stk, "eye64", [64, 64], BF16)
        eyeT6 = sb(stk, "eyeT6", [6, 12], BF16)
        eyeB6 = sb(stk, "eyeB6", [6, 12], BF16)
        selE = sb(stk, "selE", [8, 512], BF16)
        TTall = sb(stk, "TTall", [12, 12 * H], BF16)
        ones1 = sb(stk, "ones1", [1, 128], F32)
        invn = sb(stk, "invn", [128, NI], F32)
        b1c = sb(stk, "b1c", [128, 8], F32)
        b2eb = sb(stk, "b2eb", [8, 1], F32)
        rw2T = sb(stk, "rw2T", [128, 64], BF16)
        hexT = sb(stk, "hexT", [128, 48], BF16)
        q6T = sb(stk, "q6T", [128, 48], BF16)
        bfwd_s = sb(stk, "bfwd_s", [8, D], BF16)
        binv_s = sb(stk, "binv_s", [8, D], BF16)
        aI = sb(stk, "aI", [128, 128], BF16)
        stats2 = [sb(stk, f"stats{p}", [128, 2 * NI], F32) for p in range(2)]
        mrs2 = [sb(stk, f"mrs{p}", [128, NI], F32) for p in range(2)]
        l_slab = sb(stk, "l_slab", [64, 512], F32)
        rinv_s = sb(stk, "rinv_s", [64, 512], F32)
        rhl = sb(stk, "rhl", [64, 1024], BF16)
        sc = sb(stk, "sc", [128, 64], F32)
        tr736 = sb(stk, "tr736", [7, 36], F32)
        tr736s = sb(stk, "tr736s", [7, 36], F32)
        tr252 = sb(stk, "tr252", [1, 256], F32)
        tws = sb(stk, "tws", [16, 8], F32)
        mh_bf = sb(stk, "mh_bf", [16, 36], BF16)
        cols = sb(stk, "cols", [128, 8], F32)
        col6 = sb(stk, "col6", [6, 1], F32)
        CIB, CNM, CAL, CBI, CEP = 0, 1, 2, 3, 4

        pA = E(nc.psum_tensor("pA", [128, 512], F32))
        pB = E(nc.psum_tensor("pB", [128, 512], F32))
        pC = E(nc.psum_tensor("pC", [128, 512], F32))
        pD = E(nc.psum_tensor("pD", [128, 512], F32))
        pO = E(nc.psum_tensor("pO", [128, 512], F32))
        pE_ = E(nc.psum_tensor("pE", [128, 1024], F32))
        pTb = E(nc.psum_tensor("pTb", [128, 1024], BF16))
        PS = {k: bf(k) for k in ("pA", "pB", "pC", "pD", "pO", "pE", "pTb")}

        B.group("gpsimd", [], [bf(_nm(wbf_qkv))],
                [lambda g: g.dma_start(out=wbf_qkv[:, :], in_=qkv_w[:, :])], "dgq", 16)
        for dst, src in ((wbf_outw, out_w), (wbf_wq, w_q), (wbf_wk, w_k),
                         (wbf_wo, w_o), (wbf_winv, w_inv), (wbf_rw1, rw1)):
            B.group("gpsimd", [], [bf(_nm(dst))],
                    [lambda g, dst=dst, src=src: g.dma_start(out=dst[:, :], in_=src[:, :])], "dg", 16)
        for nm, src in vwu_src.items():
            dst = dr[nm]
            B.group("gpsimd", [], [bf(nm)],
                    [lambda g, dst=dst, src=src, k=k:
                     g.dma_start(out=dst[:, RR * k:RR * (k + 1)], in_=src[k, :, :]) for k in range(KEXP)],
                    "dg", 16 * KEXP)
        B.group("gpsimd", [], [bf("bfwd_s")], [lambda g: g.dma_start(out=bfwd_s[:, :], in_=b_fwd[:, :])], "dg", 16)
        B.group("gpsimd", [], [bf("binv_s")], [lambda g: g.dma_start(out=binv_s[:, :], in_=b_inv[:, :])], "dg", 16)

        st_init = contextlib.ExitStack()
        rw2bf = sb(st_init, "rw2bf", [8, D], BF16, side="right")
        hexbf = sb(st_init, "hexbf", [6, D], BF16, side="right")
        q6bf = sb(st_init, "q6bf", [6, D], BF16, side="right")
        B.group("gpsimd", [], [bf("rw2bf")], [lambda g: g.dma_start(out=rw2bf[:, :], in_=rw2[:, :])], "dg", 16)
        B.group("gpsimd", [], [bf("hexbf")], [lambda g: g.dma_start(out=hexbf[:, :], in_=hex_w[:, :])], "dg", 16)
        B.group("gpsimd", [], [bf("q6bf")], [lambda g: g.dma_start(out=q6bf[:, :], in_=q6_w[:, :])], "dg", 16)

        for dst, src in ((eye, c_eye), (eyef, c_eyef), (negI, c_negI), (maskW, c_maskW),
                         (eye64, c_eye64), (eyeT6, c_eyeT6), (eyeB6, c_eyeB6), (selE, c_selE),
                         (TTall, c_TTeye), (ones1, c_ones1), (invn, c_invn)):
            B.group("sync", [], [bf(_nm(dst))],
                    [lambda s, dst=dst, src=src: s.dma_start(out=dst[:, :], in_=src[:, :])], "ds", 16)
        B.group("sync", [], [bf("b1c")],
                [lambda s: s.dma_start(out=b1c[:, :], in_=rb1.rearrange("(m p) -> p m", p=128), allow_slow_non_contiguous=True)], "ds", 16)
        B.group("sync", [], [bf("sc_l")], [
            lambda s: s.dma_start(out=sc[0:1, 0:1], in_=lam_logit.rearrange("(a b) -> a b", a=1)),
            lambda s: s.dma_start(out=sc[0:1, 1:2], in_=scale_logit.rearrange("(a b) -> a b", a=1)),
            lambda s: s.dma_start(out=sc[0:1, 2:3], in_=sips_alpha.rearrange("(a b) -> a b", a=1)),
            lambda s: s.dma_start(out=sc[0:1, 3:4], in_=sips_beta.rearrange("(a b) -> a b", a=1)),
            lambda s: s.dma_start(out=sc[0:1, 4:5], in_=alpha_bi.rearrange("(a b) -> a b", a=1)),
            lambda s: s.dma_start(out=sc[0:8, 5:6], in_=rb2.rearrange("(k o) -> k o", o=1)),
            lambda s: s.dma_start(out=sc[0:8, 6:7], in_=expert_bias.rearrange("(k o) -> k o", o=1)),
        ], "ds", 16 * 7)
        B.group("sync", [], [bf("tr736")],
                [lambda s: s.dma_start(out=tr736[:, :], in_=transforms.rearrange("t d e -> t (d e)"))], "ds", 16)
        B.group("sync", [], [bf("tr252")],
                [lambda s: s.dma_start(out=tr252[0:1, 0:252],
                                       in_=transforms.rearrange("t d e -> (t d e)").rearrange("(a b) -> a b", a=1))],
                "ds", 16)
        B.group("sync", [], [bf("tws")],
                [lambda s: s.dma_start(out=tws[:, 0:7], in_=tw_in[:, :])], "ds", 16)
        _ds_tok = ("ds", B.sems["ds"][1], "sync")
        _dg_tok = ("dg", B.sems["dg"][1], "gpsimd")
        for _k, _b in list(bufs.items()):
            if _b.w is not None and _b.w[0] == "ds":
                _b.w = _ds_tok
            if _b.w is not None and _b.w[0] == "dg":
                _b.w = _dg_tok

        B.group("scalar", [bf("sc_l")], [bf("sc_sig")], [
            lambda s: s.activation(sc[0:1, 8:9], sc[0:1, 0:1], AF.Sigmoid),
            lambda s: s.activation(sc[0:1, 9:10], sc[0:1, 1:2], AF.Sigmoid),
        ], "ac")
        B.group("vector", [bf("sc_sig"), bf("tr252"), bf("sc_l")], [bf("sc_d"), bf("b2eb")], [
            lambda v: v.tensor_single_scalar(sc[0:1, 10:11], sc[0:1, 8:9], 0.5, ALU.mult),
            lambda v: v.tensor_single_scalar(sc[0:1, 16:17], sc[0:1, 9:10], 2.0, ALU.mult),
            lambda v: v.tensor_single_scalar(sc[0:1, 11:12], sc[0:1, 16:17], 1.0 / 6.0, ALU.mult),
            lambda v: v.reciprocal(sc[0:1, 12:13], sc[0:1, 3:4]),
            lambda v: v.scalar_tensor_tensor(sc[0:1, 13:14], sc[0:1, 2:3], -1.0, sc[0:1, 2:3], op0=ALU.mult, op1=ALU.max),
            lambda v: v.reduce_max(sc[0:1, 14:15], tr252[0:1, 0:252], axis=AX.X, apply_absolute_value=True),
            lambda v: v.tensor_tensor(sc[0:1, 17:18], sc[0:1, 14:15], sc[0:1, 16:17], ALU.mult),
            lambda v: v.tensor_single_scalar(sc[0:1, 17:18], sc[0:1, 17:18], 6.0, ALU.mult),
            lambda v: v.tensor_single_scalar(sc[0:1, 18:19], sc[0:1, 8:9], 3.0, ALU.mult),
            lambda v: v.tensor_tensor(sc[0:1, 15:16], sc[0:1, 13:14], sc[0:1, 18:19], ALU.add),
            lambda v: v.tensor_tensor(sc[0:1, 15:16], sc[0:1, 15:16], sc[0:1, 17:18], ALU.add),
            lambda v: v.tensor_scalar(sc[0:1, 15:16], sc[0:1, 15:16], -1.0, 43.0, op0=ALU.mult, op1=ALU.add),
            lambda v: v.tensor_tensor(b2eb[:, 0:1], sc[0:8, 5:6], sc[0:8, 6:7], ALU.add),
        ], "vd")

        def bcast(src_ap, dst_ap, nrow, dstbuf):
            B.group("tensor", [bf("sc_d"), bf("ones1")], [PS["pE"]],
                    [lambda t, s=src_ap, n=nrow: t.matmul(pE_[0:n, 0:1], ones1[0:1, 0:n], s,
                                                          start=True, stop=True)], "tp")
            B.group("vector", [PS["pE"]], [dstbuf],
                    [lambda v, d=dst_ap, n=nrow: v.tensor_copy(d, pE_[0:n, 0:1])], "vd")
        bcast(sc[0:1, 12:13], cols[:, CIB:CIB + 1], 128, bf("cols"))
        bcast(sc[0:1, 15:16], cols[:, CNM:CNM + 1], 128, bf("cols"))
        bcast(sc[0:1, 2:3], cols[:, CAL:CAL + 1], 128, bf("cols"))
        bcast(sc[0:1, 4:5], cols[:, CBI:CBI + 1], 128, bf("cols"))
        bcast(sc[0:1, 10:11], col6[0:6, 0:1], 6, bf("col6"))
        bcast(sc[0:1, 11:12], tws[0:7, 7:8], 7, bf("sg26"))
        B.group("vector", [], [bf("cols")],
                [lambda v: v.memset(cols[:, CEP:CEP + 1], 1e-5)], "vd")
        B.group("vector", [], [],
                [lambda v: v.memset(sc[0:1, 63:64], 0.0)], "vd")
        B.group("vector", [bf("eye"), bf("cols")], [bf("aI")],
                [lambda v: v.tensor_scalar_mul(aI[:, :], eye[:, :], cols[:, CAL:CAL + 1])], "vd")
        B.group("vector", [bf("tws")], [bf("tws_m")],
                [lambda v: v.reduce_max(sc[0:16, 20:21], tws[0:16, 0:7], axis=AX.X, negate=True)], "vd")
        B.group("scalar", [bf("tws_m")], [bf("tws_e")],
                [lambda s: s.activation(sc[0:16, 24:31], tws[0:16, 0:7], AF.Exp, bias=sc[0:16, 20:21])], "ac")
        B.group("vector", [bf("tws_e")], [bf("tws_w")], [
            lambda v: v.reduce_sum(sc[0:16, 21:22], sc[0:16, 24:31], axis=AX.X),
            lambda v: v.reciprocal(sc[0:16, 22:23], sc[0:16, 21:22]),
            lambda v: v.tensor_scalar_mul(tws[0:16, 0:7], sc[0:16, 24:31], sc[0:16, 22:23]),
        ], "vd")
        B.group("vector", [bf("tr736"), bf("sg26")], [bf("tr736s")],
                [lambda v: v.tensor_scalar_mul(tr736s[:, :], tr736[:, :], tws[0:7, 7:8])], "vd")
        B.group("tensor", [bf("tws_w"), bf("eyef")], [PS["pE"]],
                [lambda t: t.transpose(pE_[0:7, 16:32], tws[0:16, 0:7], eyef[0:16, 0:16])], "tp")
        B.group("vector", [PS["pE"]], [bf("wT")],
                [lambda v: v.tensor_copy(sc[0:7, 32:48], pE_[0:7, 16:32])], "vd")
        B.group("tensor", [bf("wT"), bf("tr736s")], [PS["pE"]],
                [lambda t: t.matmul(pE_[0:16, 32:68], sc[0:7, 32:48], tr736s[0:7, 0:36],
                                    start=True, stop=True)], "tp")
        B.group("vector", [PS["pE"]], [bf("mh_bf")],
                [lambda v: v.tensor_copy(mh_bf[:, :], pE_[0:16, 32:68])], "vd")
        B.group("sync", [bf("mh_bf")], [bf("mh_dram")],
                [lambda s: s.dma_start(out=mh_dram[:, :], in_=mh_bf[:, :])], "ds", 16)
        B.group("sync", [bf("mh_dram"), bf("TTall")], [bf("TTall")],
                [lambda s, h=h: s.dma_start(out=TTall[6:12, 12 * h + 6:12 * h + 12],
                                            in_=mh_dram[h, :].rearrange("(d e) -> d e", d=6))
                 for h in range(H)], "ds", 16 * H)
        for m in range(8):
            B.group("tensor", [bf("rw2bf"), bf("eye")], [PS["pTb"]],
                    [lambda t, m=m: t.transpose(pTb[:, 8 * m:8 * (m + 1)],
                                                rw2bf[:, 128 * m:128 * (m + 1)], eye[0:8, 0:8])], "tp")
        B.group("vector", [PS["pTb"]], [bf("rw2T")],
                [lambda v: v.tensor_copy(rw2T[:, :], pTb[:, 0:64])], "vd")
        for src_s, dst_s, key in ((hexbf, hexT, "hexT"), (q6bf, q6T, "q6T")):
            for j in range(8):
                B.group("tensor", [bf(_nm(src_s)), bf("eye")], [PS["pTb"]],
                        [lambda t, src_s=src_s, j=j: t.transpose(pTb[:, 64 + 6 * j:64 + 6 * (j + 1)],
                                                                 src_s[:, 128 * j:128 * (j + 1)],
                                                                 eye[0:6, 0:6])], "tp")
            B.group("vector", [PS["pTb"]], [bf(key)],
                    [lambda v, dst_s=dst_s: v.tensor_copy(dst_s[:, :], pTb[:, 64:112])], "vd")
        B.fence()
        st_init.close()
        B.fence()

        xb = [sb(stk, f"xb{p}", [128, 1024], F32, side="right") for p in range(2)]
        xt_tm = sb(stk, "xt_tm", [128, 1024], BF16, side="right")
        xsq = sb(stk, "xsq", [128, 1024], F32, side="right")
        XBb = [bf("xb0"), bf("xb1")]
        XT = bf("xt_tm")

        xb2bufs = ((0, "sxb0"), (1, "sxb1"))

        def em_ln_chunk(s_j, st_t, ST_B, i, bufs=xb2bufs, xbl=None):
            bi, semn = bufs[i % len(bufs)]
            xbp = (xbl or xb)[bi]
            XBp = bf(_nm(xbp))
            B.group("sync", [XBp], [XBp],
                    [lambda s, s_j=s_j, i=i, xbp=xbp: s.dma_start(
                        out=xbp[:, :], in_=x_in[s_j, 128 * i:128 * (i + 1), :])], semn, 16)
            B.group("vector", [XBp], [bf("xsq"), ST_B], [
                lambda v, xbp=xbp: v.tensor_tensor(xsq[:, :], xbp[:, :], xbp[:, :], ALU.mult),
                lambda v, xbp=xbp: v.reduce_sum(sc[:, 49:50], xbp[:, :], axis=AX.X),
                lambda v: v.reduce_sum(sc[:, 50:51], xsq[:, :], axis=AX.X),
                lambda v, i=i, st_t=st_t: v.tensor_single_scalar(st_t[:, 2 * i:2 * i + 1], sc[:, 49:50], 1.0 / D, ALU.mult),
                lambda v: v.tensor_single_scalar(sc[:, 51:52], sc[:, 50:51], 1.0 / D, ALU.mult),
                lambda v, i=i, st_t=st_t: v.tensor_tensor(sc[:, 52:53], st_t[:, 2 * i:2 * i + 1],
                                               st_t[:, 2 * i:2 * i + 1], ALU.mult),
                lambda v, i=i, st_t=st_t: v.tensor_tensor(st_t[:, 2 * i + 1:2 * i + 2], sc[:, 51:52],
                                               sc[:, 52:53], ALU.subtract),
            ], "vd")

        for s_i in range(SEQ):
            st_seq = contextlib.ExitStack()
            stats = stats2[s_i % 2]
            mrs = mrs2[s_i % 2]
            STATS = bf(f"stats{s_i % 2}")
            q_slab = sb(st_seq, f"q_slab{s_i}", [128, 8 * T], BF16, side="right")
            k_slab = sb(st_seq, f"k_slab{s_i}", [128, 8 * T], BF16, side="right")
            v_ext = sb(st_seq, f"v_ext{s_i}", [128, NI * 16 * 65], BF16, side="right")
            lhs0_t = sb(st_seq, f"lhs0_t{s_i}", [12, T], BF16, side="right")
            lhsL_t = sb(st_seq, f"lhsL_t{s_i}", [128, T], BF16, side="right")
            QS, KS, VS = Buf(), Buf(), Buf()
            SBT, SBL, HXC, L0, LL = Buf(), Buf(), Buf(), Buf(), Buf()
            CRHb = [Buf(), Buf()]
            TSBb = [Buf(), Buf()]
            PSBb = [Buf(), Buf(), Buf()]
            SSBb = [Buf()]
            RB128, LSL = Buf(), Buf()
            TMPL = [Buf(), Buf()]
            B.group("vector", [], [LL], [lambda v: v.memset(lhsL_t[:, :], 0.0)], "vd")
            qr = q_slab.rearrange("p (m t) -> p m t", t=T)
            krr = k_slab.rearrange("p (m t) -> p m t", t=T)
            ver = v_ext.rearrange("p (jc h c) -> p jc h c", h=16, c=65)

            st_ln = contextlib.ExitStack()
            xn_slab = sb(st_ln, f"xn{s_i}", [128, 8 * T], BF16, side="right")
            sb_t = sb(st_ln, f"sb_t{s_i}", [6, T], BF16, side="right")
            sbL_t = sb(st_ln, f"sbL_t{s_i}", [6, T], BF16, side="right")
            hexc_t = sb(st_ln, f"hexc_t{s_i}", [6, T], BF16, side="right")
            XNS = Buf()
            xnr = xn_slab.rearrange("p (d t) -> p d t", t=T)

            def em_ln_norm(i, bufs=xb2bufs, xbl=None):
                bi, _semn = bufs[i % len(bufs)]
                xbp = (xbl or xb)[bi]
                B.group("scalar", [bf(_nm(xbp)), STATS, bf(f"mrs{s_i % 2}")], [XT],
                        [lambda s, i=i, xbp=xbp, stats=stats, mrs=mrs: s.activation(
                            xt_tm[:, :], xbp[:, :], AF.Identity,
                            bias=mrs[:, i:i + 1], scale=stats[:, 2 * i + 1:2 * i + 2])], "ac")
                B.group("tensor", [XT, bf("eye")], [PS["pTb"]],
                        [lambda t, d=d: t.transpose(pTb[:, 128 * d:128 * (d + 1)],
                                                    xt_tm[:, 128 * d:128 * (d + 1)], eye[:, :])
                         for d in range(8)], "tp")
                if DBG and s_i == 0 and i == 0:
                    B.group("sync", [XT], [], [lambda s: s.dma_start(out=dbg_o["d_xt"][:, :], in_=xt_tm[:, :])], "ds", 16)
                    B.group("vector", [PS["pTb"]], [bf("xsq")],
                            [lambda v: v.tensor_copy(xsq[:, :], pTb[:, :])], "vd")
                    B.group("sync", [bf("xsq")], [], [lambda s: s.dma_start(out=dbg_o["d_ptb"][:, :], in_=xsq[:, :])], "ds", 16)
                B.group("vector", [PS["pTb"]], [XNS],
                        [lambda v, i=i: v.tensor_copy(xnr[:, :, 128 * i:128 * (i + 1)],
                                                      pTb.rearrange("p (d t) -> p d t", t=128))], "vd")

            st_qk = contextlib.ExitStack()
            qTs = [sb(st_qk, f"qT{s_i}", [128, 8192], BF16, side="right"),
                   v_ext[:, 0:8192] if NI * 16 * 65 >= 8192
                   else sb(st_qk, f"qTb{s_i}", [128, 8192], BF16, side="right")]
            QTb = [Buf(), Buf()]

            def em_qT_load(half):
                B.group("sync", [bf("wbf_qkv")], [QTb[half]],
                        [lambda s, j=j, half=half: s.dma_start(
                            out=qTs[half][:, 1024 * j:1024 * (j + 1)],
                            in_=wbf_qkv[1024 * half:1024 * (half + 1), 128 * j:128 * (j + 1)],
                            transpose=True) for j in range(8)], "sqT" if half == 0 else "sqT1", 16 * 8)

            xbE = [sb(st_qk, f"xbE{s_i}{k}", [128, 1024], F32, side="right") for k in range(2)]
            xb4 = [xb[0], xb[1], xbE[0], xbE[1]]
            bufs4 = ((0, "sxb0"), (1, "sxb1"), (2, "sxb2"), (3, "sxb3"))
            if s_i == 0:
                for i in range(NI):
                    em_ln_chunk(s_i, stats, STATS, i, bufs=bufs4, xbl=xb4)
            if DBG and s_i == 0:
                B.group("sync", [STATS], [], [lambda s, stats=stats: s.dma_start(out=dbg_o["d_var"][:, :], in_=stats[:, :])], "ds", 16)
            vview = stats.rearrange("p (i two) -> p two i", two=2)[:, 1, :]
            B.group("scalar", [STATS, bf("cols")], [bf("rstd0")],
                    [lambda s, vview=vview: s.activation(sc[:, 32:32 + NI], vview, AF.Sqrt, bias=cols[:, CEP:CEP + 1])], "ac")
            mview = stats.rearrange("p (i two) -> p two i", two=2)[:, 0, :]
            B.group("vector", [bf("rstd0")], [STATS, bf(f"mrs{s_i % 2}")], [
                lambda v, vview=vview: v.reciprocal(vview, sc[:, 32:32 + NI]),
                lambda v, vview=vview, mview=mview, mrs=mrs: v.tensor_tensor(mrs[:, :], mview, vview, ALU.mult),
                lambda v, mrs=mrs: v.tensor_single_scalar(mrs[:, :], mrs[:, :], -1.0, ALU.mult),
            ], "vd")
            for i in range(NI):
                bi, semn = bufs4[i % 4]
                xbp = xb4[bi]
                B.group("sync", [bf(_nm(xbp))], [bf(_nm(xbp))],
                        [lambda s, s_i=s_i, i=i, xbp=xbp: s.dma_start(
                            out=xbp[:, :], in_=x_in[s_i, 128 * i:128 * (i + 1), :])], semn, 16)
                em_ln_norm(i, bufs=bufs4, xbl=xb4)
                if i == 1:
                    em_qT_load(0)
                    em_qT_load(1)
            def tail_hex(tc):
                B.group("tensor", [bf("hexT"), XNS], [PS["pC"]],
                        [lambda t, j=j, tc=tc: t.matmul(
                            pC[0:6, :], hexT[:, 6 * j:6 * (j + 1)], xnr[:, j, 512 * tc:512 * (tc + 1)],
                            start=(j == 0), stop=(j == 7)) for j in range(8)], "tp")
                B.group("scalar", [PS["pC"]], [SBT],
                        [lambda s, tc=tc: s.activation(sb_t[:, 512 * tc:512 * (tc + 1)], pC[0:6, :], AF.Tanh)], "ac")
                B.group("vector", [SBT, bf("col6")], [SBL],
                        [lambda v, tc=tc: v.tensor_scalar_mul(sbL_t[:, 512 * tc:512 * (tc + 1)],
                                                              sb_t[:, 512 * tc:512 * (tc + 1)], col6[0:6, 0:1])], "vd")

            def tail_q6(tc):
                B.group("tensor", [bf("q6T"), XNS], [PS["pD"]],
                        [lambda t, j=j, tc=tc: t.matmul(
                            pD[0:6, :], q6T[:, 6 * j:6 * (j + 1)], xnr[:, j, 512 * tc:512 * (tc + 1)],
                            start=(j == 0), stop=(j == 7)) for j in range(8)], "tp")
                B.group("scalar", [PS["pD"]], [HXC],
                        [lambda s, tc=tc: s.activation(hexc_t[:, 512 * tc:512 * (tc + 1)], pD[0:6, :], AF.Tanh)], "ac")

            def tail_asm0(tc):
                B.group("tensor", [SBT, HXC, bf("eyeT6"), bf("eyeB6")], [PS["pC"]], [
                    lambda t, tc=tc: t.matmul(pC[0:12, :], eyeT6[:, :], sb_t[:, 512 * tc:512 * (tc + 1)],
                                              start=True, stop=False),
                    lambda t, tc=tc: t.matmul(pC[0:12, :], eyeB6[:, :], hexc_t[:, 512 * tc:512 * (tc + 1)],
                                              start=False, stop=True)], "tp")
                B.group("vector", [PS["pC"]], [L0],
                        [lambda v, tc=tc: v.tensor_copy(lhs0_t[:, 512 * tc:512 * (tc + 1)], pC[0:12, :])], "vd")

            def tail_asmL(tc):
                B.group("tensor", [SBL, HXC, bf("eyeT6"), bf("eyeB6")], [PS["pD"]], [
                    lambda t, tc=tc: t.matmul(pD[0:12, :], eyeT6[:, :], sbL_t[:, 512 * tc:512 * (tc + 1)],
                                              start=True, stop=False),
                    lambda t, tc=tc: t.matmul(pD[0:12, :], eyeB6[:, :], hexc_t[:, 512 * tc:512 * (tc + 1)],
                                              start=False, stop=True)], "tp")
                B.group("vector", [PS["pD"]], [LL],
                        [lambda v, tc=tc: v.tensor_copy(lhsL_t[0:12, 512 * tc:512 * (tc + 1)], pD[0:12, :])], "vd")

            tail_q = [f for tc in range(TC) for f in (lambda tc=tc: tail_hex(tc), lambda tc=tc: tail_q6(tc))]
            tail_q += [f for tc in range(TC) for f in (lambda tc=tc: tail_asm0(tc), lambda tc=tc: tail_asmL(tc))]

            for half in range(2):
                for m in range(8):
                    for tc in range(TC):
                        pp, ppB = (pA, PS["pA"]) if (m * TC + tc) % 2 == 0 else (pB, PS["pB"])
                        B.group("tensor", [QTb[half], XNS], [ppB],
                                [lambda t, j=j, m=m, tc=tc, pp=pp, half=half: t.matmul(
                                    pp[:, :], qTs[half][:, 1024 * j + 128 * m:1024 * j + 128 * (m + 1)],
                                    xnr[:, j, 512 * tc:512 * (tc + 1)],
                                    start=(j == 0), stop=(j == 7)) for j in range(8)], "tp")
                        if half == 0:
                            B.group("scalar", [ppB, bf("cols")], [QS],
                                    [lambda s, m=m, tc=tc, pp=pp: s.activation(
                                        qr[:, m, 512 * tc:512 * (tc + 1)], pp[:, :], AF.Identity,
                                        scale=cols[:, CIB:CIB + 1])], "ac")
                        else:
                            B.group("vector", [ppB], [KS],
                                    [lambda v, m=m, tc=tc, pp=pp: v.tensor_copy(
                                        krr[:, m, 512 * tc:512 * (tc + 1)], pp[:, :])], "vd")
                            if (m * TC + tc) % 2 == 0 and tail_q:
                                tail_q.pop(0)()
            for f in tail_q:
                f()
            B.fence(("sync", "gpsimd"))
            st_qk.close()

            st_v = contextlib.ExitStack()
            vT = sb(st_v, f"vT{s_i}", [128, 8192], BF16, side="right")
            VT = Buf()
            B.group("sync", [bf("wbf_qkv")], [VT],
                    [lambda s, j=j: s.dma_start(out=vT[:, 1024 * j:1024 * (j + 1)],
                                                in_=wbf_qkv[2048:3072, 128 * j:128 * (j + 1)],
                                                transpose=True) for j in range(8)], "svT", 16 * 8)
            B.group("vector", [], [VS], [lambda v: v.memset(v_ext[:, :], 1.0)], "vd")
            for jc in range(NI):
                B.group("tensor", [VT, XNS], [PS["pE"]],
                        [lambda t, j=j, jc=jc, hh=hh: t.matmul(
                            pE_[:, 512 * hh:512 * (hh + 1)], xnr[:, j, 128 * jc:128 * (jc + 1)],
                            vT[:, 1024 * j + 512 * hh:1024 * j + 512 * (hh + 1)],
                            start=(j == 0), stop=(j == 7))
                         for hh in range(2) for j in range(8)], "tp")
                B.group("vector", [PS["pE"]], [VS],
                        [lambda v, jc=jc: v.tensor_copy(ver[:, jc, :, 0:64],
                                                        pE_.rearrange("p (h c) -> p h c", c=64))], "vd")
            if DBG and s_i == 0:
                B.group("sync", [STATS], [], [lambda s, stats=stats: s.dma_start(out=dbg_o["d_stats"][:, :], in_=stats[:, :])], "ds", 16)
                B.group("sync", [XNS], [], [lambda s: s.dma_start(out=dbg_o["d_xn"][:, :], in_=xn_slab[:, :])], "ds", 16)
                B.group("sync", [QS], [], [lambda s: s.dma_start(out=dbg_o["d_q"][:, :], in_=q_slab[:, :])], "ds", 16)
                B.group("sync", [KS], [], [lambda s: s.dma_start(out=dbg_o["d_k"][:, :], in_=k_slab[:, :])], "ds", 16)
                B.group("sync", [VS], [], [lambda s: s.dma_start(out=dbg_o["d_vext"][:, :], in_=v_ext[:, :])], "ds", 16)
                B.group("sync", [L0], [], [lambda s: s.dma_start(out=dbg_o["d_lhs0"][:, :], in_=lhs0_t[:, :])], "ds", 16)
                B.group("sync", [LL], [], [lambda s: s.dma_start(out=dbg_o["d_lhsL"][:, :], in_=lhsL_t[:, :])], "ds", 16)
            B.fence(("sync", "gpsimd"))
            st_v.close()
            B.fence(("sync", "gpsimd"))
            st_ln.close()

            st_fl = contextlib.ExitStack()
            crh = [sb(st_fl, f"crh{s_i}{p}", [128, T], BF16, side="right") for p in range(2)]
            t_sb = [sb(st_fl, f"t_sb{s_i}{p}", [128, 512], BF16, side="right") for p in range(2)]
            s_sb = [sb(st_fl, f"s_sb{s_i}0", [128, 2048], F32, side="right")]
            P_sb = [sb(st_fl, f"P_sb{s_i}{p}", [128, 2048], BF16, side="right") for p in range(3)]
            rb128 = sb(st_fl, f"rb128{s_i}", [128, 512], F32, side="right")
            tmp_l = [rb128[0:1, :], rb128[32:33, :]]
            st_out = contextlib.ExitStack()
            attnout = sb(st_out, f"attnout{s_i}", [128, 8 * T], BF16, side="left")
            ATT = Buf()
            B.group("vector", [], [LSL], [lambda v: v.memset(l_slab[:, :], 1.0)], "vd")
            B.group("vector", [], [CRHb[0]], [lambda v: v.memset(crh[0][:, :], 0.0)], "vd")
            B.group("vector", [], [CRHb[1]], [lambda v: v.memset(crh[1][:, :], 0.0)], "vd")
            ar = attnout.rearrange("p (m t) -> p m t", t=T)
            tasks = []
            for h in range(H):
                for ic in range(NICH):
                    njj = 4 * ic + 4
                    for jj in range(njj):
                        tasks.append((h, ic, jj, jj == 0, jj == njj - 1))
            NT_ = len(tasks)
            for _k2 in ("pE0", "pE1"):
                bf(_k2).w = PS["pE"].w
                bf(_k2).readers = list(PS["pE"].readers)
            CBK = [(pC, PS["pC"]), (pD, PS["pD"]), (pE_, bf("pE0"))]
            AVB = [(pO, PS["pO"], 0), (pE_, bf("pE1"), 512)]

            def em_crh(h, tcs=None):
                for tc in (range(TC) if tcs is None else tcs):
                    B.group("tensor", [L0, bf("TTall")], [PS["pC"]],
                            [lambda t, h=h, tc=tc: t.matmul(pC[0:12, :], TTall[:, 12 * h:12 * (h + 1)],
                                                            lhs0_t[:, 512 * tc:512 * (tc + 1)],
                                                            start=True, stop=True)], "tp")
                    B.group("vector", [PS["pC"]], [CRHb[h % 2]],
                            [lambda v, h=h, tc=tc: v.tensor_copy(crh[h % 2][0:12, 512 * tc:512 * (tc + 1)],
                                                                 pC[0:12, :])], "vd")

            def em_raw(idx):
                h, ic, jj, _, _ = tasks[idx]
                p = idx % 2
                hp, rbo = h // 2, 64 * (h % 2)
                pR, pRB = (pA, PS["pA"]) if p == 0 else (pB, PS["pB"])
                B.group("tensor", [QS, KS], [pRB],
                        [lambda t, hp=hp, rbo=rbo, jj=jj, ic=ic, pR=pR: t.matmul(
                            pR[:, :], krr[rbo:rbo + 64, hp, 128 * jj:128 * (jj + 1)],
                            qr[rbo:rbo + 64, hp, 512 * ic:512 * (ic + 1)],
                            start=True, stop=True)], "tp")

            def em_combo(idx):
                h, ic, jj, _, _ = tasks[idx]
                pCc, pCB = CBK[idx % 3]
                B.group("tensor", [LL, CRHb[h % 2]], [pCB],
                        [lambda t, h=h, jj=jj, ic=ic, pCc=pCc: t.matmul(
                            pCc[:, 0:512], lhsL_t[:, 128 * jj:128 * (jj + 1)],
                            crh[h % 2][:, 512 * ic:512 * (ic + 1)], start=True, stop=True)], "tp")

            def em_tanh(idx):
                p = idx % 2
                pR = pA if p == 0 else pB
                pRB = PS["pA"] if p == 0 else PS["pB"]
                B.group("scalar", [pRB], [TSBb[p]],
                        [lambda s, p=p, pR=pR: s.activation(t_sb[p][:, :], pR[:, :], AF.Tanh)], "ac")

            def em_combine(idx):
                p = idx % 2
                pCc, pCB = CBK[idx % 3]
                qd = 0
                col = 512 * (idx % 4)
                B.group("vector", [TSBb[p], pCB, bf("cols")], [SSBb[qd]],
                        [lambda v, p=p, pCc=pCc, qd=qd, col=col: v.scalar_tensor_tensor(
                            s_sb[qd][:, col:col + 512], t_sb[p][:, :], cols[:, CAL:CAL + 1],
                            pCc[:, 0:512], op0=ALU.mult, op1=ALU.add)], "vd")

            def em_exp(q, nsub):
                qd = 0
                pq = q % 3
                B.group("scalar", [SSBb[qd], bf("cols")], [PSBb[pq]],
                        [lambda s, qd=qd, pq=pq, nsub=nsub: s.activation(
                            P_sb[pq][:, 0:512 * nsub], s_sb[qd][:, 0:512 * nsub], AF.Exp,
                            bias=cols[:, CNM:CNM + 1])], "ac")

            def em_mask(idx):
                h, ic, jj, _, _ = tasks[idx]
                if jj < 4 * ic:
                    return
                c = jj - 4 * ic
                pq = (idx // 4) % 3
                col = 512 * (idx % 4)
                B.group("vector", [PSBb[pq], bf("maskW")], [PSBb[pq]],
                        [lambda v, pq=pq, col=col, c=c: v.tensor_tensor(
                            P_sb[pq][:, col:col + 512], P_sb[pq][:, col:col + 512],
                            maskW[:, 384 - 128 * c:896 - 128 * c], ALU.mult)], "vd", 1)

            gctr = [0]
            def em_av(idx):
                h, ic, jj, first, last = tasks[idx]
                pq = (idx // 4) % 3
                col = 512 * (idx % 4)
                gp_ = gctr[0] % 2
                pv, pvB, off = AVB[gp_]
                B.group("tensor", [PSBb[pq], VS], [pvB],
                        [lambda t, h=h, jj=jj, pq=pq, col=col, first=first, last=last, pv=pv, off=off: t.matmul(
                            pv[0:65, off:off + 512], ver[:, jj, h, :], P_sb[pq][:, col:col + 512],
                            start=first, stop=last)], "tp")
                if last:
                    hp, rbo = h // 2, 64 * (h % 2)
                    B.group("vector", [pvB], [ATT, TMPL[gp_]], [
                        lambda v, hp=hp, rbo=rbo, ic=ic, pv=pv, off=off: v.tensor_copy(
                            ar[rbo:rbo + 64, hp, 512 * ic:512 * (ic + 1)], pv[0:64, off:off + 512]),
                        lambda v, pv=pv, off=off, gp_=gp_: v.tensor_copy(tmp_l[gp_][0:1, :], pv[64:65, off:off + 512]),
                    ], "vd")
                    B.group("sync", [TMPL[gp_]], [LSL],
                            [lambda s, h=h, ic=ic, gp_=gp_: s.dma_start(
                                out=l_slab[NICH * h + ic:NICH * h + ic + 1, :], in_=tmp_l[gp_][0:1, :])],
                            f"stl{gp_}", 16)
                    gctr[0] += 1

            em_crh(0)
            if H > 1:
                em_crh(1)
            crh_q = []
            for idx in range(NT_ + 8):
                if idx < NT_:
                    h, ic, jj, first, last = tasks[idx]
                    if first and ic == 0 and h + 2 < H:
                        crh_q.extend((h + 2, tc) for tc in range(TC))
                    em_raw(idx)
                    em_tanh(idx)
                    em_combo(idx)
                if s_i + 1 < SEQ and idx % 38 == 19 and idx // 38 < NI:
                    em_ln_chunk(s_i + 1, stats2[(s_i + 1) % 2], bf(f"stats{(s_i + 1) % 2}"), idx // 38)
                j1 = idx - 1
                if 0 <= j1 < NT_:
                    em_combine(j1)
                    if j1 % 3 == 1 and crh_q:
                        hh2, tc2 = crh_q.pop(0)
                        em_crh(hh2, [tc2])
                    if j1 % 4 == 3:
                        em_exp(j1 // 4, 4)
                    elif j1 == NT_ - 1:
                        em_exp(j1 // 4, (j1 % 4) + 1)
                j4m = idx - 4
                if 0 <= j4m < NT_:
                    em_mask(j4m)
                j6 = idx - 8
                if 0 <= j6 < NT_:
                    em_av(j6)
            B.wait_tok("vector", ("stl0", B.sems["stl0"][1], "sync"))
            B.wait_tok("vector", ("stl1", B.sems["stl1"][1], "sync"))
            B.group("vector", [LSL], [bf("rinv"), bf("rhl")], [
                lambda v: v.reciprocal(rinv_s[:, :], l_slab[:, :]),
                lambda v: v.tensor_copy(rhl[:, 0:512], rinv_s[:, :]),
                lambda v: v.tensor_tensor(rhl[:, 512:1024], rinv_s[:, :], rhl[:, 0:512], ALU.subtract),
            ], "vd")
            for h in range(H):
                hp, rbo = h // 2, 64 * (h % 2)
                for ic in range(NICH):
                    r = NICH * h + ic
                    B.group("tensor", [bf("rhl"), bf("eye64")], [PS["pE"]], [
                        lambda t, r=r: t.matmul(pE_[0:64, 0:512], eye64[:, r:r + 1].broadcast_to([64, 64]),
                                                rhl[:, 0:512], start=True, stop=False),
                        lambda t, r=r: t.matmul(pE_[0:64, 0:512], eye64[:, r:r + 1].broadcast_to([64, 64]),
                                                rhl[:, 512:1024], start=False, stop=True)], "tp")
                    B.group("vector", [PS["pE"]], [RB128],
                            [lambda v, rbo=rbo: v.tensor_copy(rb128[rbo:rbo + 64, :], pE_[0:64, 0:512])], "vd")
                    B.group("vector", [RB128], [ATT],
                            [lambda v, hp=hp, rbo=rbo, ic=ic: v.tensor_tensor(
                                ar[rbo:rbo + 64, hp, 512 * ic:512 * (ic + 1)],
                                ar[rbo:rbo + 64, hp, 512 * ic:512 * (ic + 1)],
                                rb128[rbo:rbo + 64, :], ALU.mult)], "vd")
            if DBG and s_i == 0:
                B.group("sync", [ATT], [], [lambda s: s.dma_start(out=dbg_o["d_att"][:, :], in_=attnout[:, :])], "ds", 16)
                B.group("sync", [CRHb[1]], [], [lambda s: s.dma_start(out=dbg_o["d_crh0"][:, :], in_=crh[1][:, :])], "ds", 16)
                B.group("sync", [bf("rinv")], [], [lambda s: s.dma_start(out=dbg_o["d_lsl"][:, :], in_=l_slab[:, :])], "ds", 16)
            B.fence()
            st_fl.close()
            st_seq.close()

            st_ffn = contextlib.ExitStack()
            ffnout = sb(st_out, f"ffnout{s_i}", [128, 8 * T], BF16, side="left")
            FFO = Buf()
            fr = ffnout.rearrange("p (m t) -> p m t", t=T)
            wp = [sb(st_ffn, f"wp{s_i}{p}", [128, 8192], BF16, side="right") for p in range(2)]
            WP = [Buf(), Buf()]
            vwuS = sb(st_ffn, f"vwuS{s_i}", [128, 3 * 4096], BF16, side="right")
            VWU = [Buf(), Buf(), Buf()]
            xn_c = sb(st_ffn, f"xnc{s_i}", [128, 4096], BF16, side="right")
            qc_s = sb(st_ffn, f"qc{s_i}", [128, 4096], BF16, side="right")
            kc_s = sb(st_ffn, f"kc{s_i}", [128, 4096], BF16, side="right")
            iqc_s = sb(st_ffn, f"iqc{s_i}", [128, 4096], BF16, side="right")
            ikc_s = sb(st_ffn, f"ikc{s_i}", [128, 4096], BF16, side="right")
            cum_s = sb(st_ffn, f"cum{s_i}", [128, 512], F32, side="right")
            ptmp = sb(st_ffn, f"ptmp{s_i}", [128, 512], BF16, side="right")
            prod_s = sb(st_ffn, f"prod{s_i}", [128, 2048], BF16, side="right")
            hbufs = [sb(st_ffn, f"hb{s_i}0", [128, 512], BF16, side="right"), ptmp]
            lg_sb = sb(st_ffn, f"lg{s_i}", [8, 512], F32, side="right")
            wfm = sb(st_ffn, f"wfm{s_i}", [8, 1024], BF16, side="right")
            prefix = sb(st_ffn, f"prefix{s_i}", [128, 8], F32, side="right")
            wtm = sb(st_ffn, f"wtm{s_i}", [128, 32], BF16, side="right")
            XNC, QC, KC, IQC, IKC = Buf(), Buf(), Buf(), Buf(), Buf()
            CUM, PTMP, PROD, LG, PFX, WTM = Buf(), Buf(), Buf(), Buf(), Buf(), Buf()
            WFMb = [Buf(), Buf()]
            HBb = [Buf(), Buf()]
            B.group("vector", [], [PFX], [lambda v: v.memset(prefix[:, :], 0.0)], "vd")

            def load_wp(p, src_dram):
                B.group("sync", [bf(_nm(src_dram))], [WP[p]],
                        [lambda s, j=j, p=p, src_dram=src_dram: s.dma_start(
                            out=wp[p][:, 1024 * j:1024 * (j + 1)],
                            in_=src_dram[:, 128 * j:128 * (j + 1)], transpose=True)
                         for j in range(8)], f"swp{p}", 16 * 8)

            def proj(wbuf_p, rhs_slab, rhs_buf, dst_r, dst_buf, evac_eng):
                for m in range(8):
                    pp, ppB = (pA, PS["pA"]) if m % 2 == 0 else (pB, PS["pB"])
                    B.group("tensor", [WP[wbuf_p], rhs_buf], [ppB],
                            [lambda t, j=j, m=m, pp=pp, wbuf_p=wbuf_p, rhs_slab=rhs_slab: t.matmul(
                                pp[:, :], wp[wbuf_p][:, 1024 * j + 128 * m:1024 * j + 128 * (m + 1)],
                                rhs_slab.rearrange("p (d t) -> p d t", t=512)[:, j, :],
                                start=(j == 0), stop=(j == 7)) for j in range(8)], "tp")
                    if evac_eng == "vector":
                        B.group("vector", [ppB], [dst_buf],
                                [lambda v, m=m, pp=pp, dst_r=dst_r: v.tensor_copy(
                                    dst_r.rearrange("p (d t) -> p d t", t=512)[:, m, :], pp[:, :])], "vd")
                    else:
                        B.group("scalar", [ppB], [dst_buf],
                                [lambda s, m=m, pp=pp, dst_r=dst_r: s.copy(
                                    dst_r.rearrange("p (d t) -> p d t", t=512)[:, m, :], pp[:, :])], "ac")

            def prewarm(func):
                B.group("scalar", [], [],
                        [lambda s, func=func: s.activation(sc[0:1, 62:63], sc[0:1, 63:64], func)], "ac")

            def em_rebuild_dma(tc, ss):
                i = 4 * tc + ss
                p = ss % 2
                B.group("sync", [XBb[p]], [XBb[p]],
                        [lambda s, s_i=s_i, i=i, p=p: s.dma_start(
                            out=xb[p][:, :], in_=x_in[s_i, 128 * i:128 * (i + 1), :])], f"sxb{p}", 16)

            def em_rebuild_id(tc, ss):
                i = 4 * tc + ss
                p = ss % 2
                B.group("scalar", [XBb[p], STATS, bf(f"mrs{s_i % 2}")], [XT],
                        [lambda s2, i=i, p=p, stats=stats, mrs=mrs: s2.activation(
                            xt_tm[:, :], xb[p][:, :], AF.Identity,
                            bias=mrs[:, i:i + 1], scale=stats[:, 2 * i + 1:2 * i + 2])], "ac")

            def em_rebuild_tp(tc, ss):
                B.group("tensor", [XT, bf("eye")], [PS["pTb"]],
                        [lambda t, d=d: t.transpose(pTb[:, 128 * d:128 * (d + 1)],
                                                    xt_tm[:, 128 * d:128 * (d + 1)], eye[:, :])
                         for d in range(8)], "tp")
                B.group("vector", [PS["pTb"]], [XNC],
                        [lambda v, ss=ss: v.tensor_copy(
                            xn_c.rearrange("p (d t) -> p d t", t=512)[:, :, 128 * ss:128 * (ss + 1)],
                            pTb.rearrange("p (d t) -> p d t", t=128))], "vd")

            for ss in range(4):
                em_rebuild_dma(0, ss)
                em_rebuild_id(0, ss)
                em_rebuild_tp(0, ss)
            for tc in range(TC):
                load_wp(0, wbf_wq)
                load_wp(1, wbf_wk)
                proj(0, xn_c, XNC, qc_s, QC, "vector")
                load_wp(0, wbf_winv)
                proj(1, xn_c, XNC, kc_s, KC, "scalar")
                load_wp(1, wbf_rw1)
                proj(0, qc_s, QC, iqc_s, IQC, "vector")
                prewarm(AF.Gelu)
                proj(0, kc_s, KC, ikc_s, IKC, "scalar")
                xtv = xt_tm.rearrange("p (a c) -> p a c", a=8)

                def em_router_pre(br, hook_m=None, hook=None):
                    src_slab, src_buf = (qc_s, QC) if br == 0 else (iqc_s, IQC)

                    def em_logits(m):
                        hb = m % 2
                        B.group("tensor", [HBb[hb], bf("rw2T")], [PS["pD"]],
                                [lambda t, m=m, hb=hb: t.matmul(pD[0:8, :], rw2T[:, 8 * m:8 * (m + 1)],
                                                                hbufs[hb][:, :], start=(m == 0), stop=(m == 7))], "tp")

                    for m in range(8):
                        hb = m % 2
                        pp, ppB = (pC, PS["pC"]) if hb == 0 else (pA, PS["pA"])
                        B.group("tensor", [WP[1], src_buf], [ppB],
                                [lambda t, j=j, m=m, pp=pp, src_slab=src_slab: t.matmul(
                                    pp[:, :], wp[1][:, 1024 * j + 128 * m:1024 * j + 128 * (m + 1)],
                                    src_slab.rearrange("p (d t) -> p d t", t=512)[:, j, :],
                                    start=(j == 0), stop=(j == 7)) for j in range(8)], "tp")
                        B.group("scalar", [ppB, bf("b1c")], [HBb[hb]],
                                [lambda s, m=m, hb=hb, pp=pp: s.activation(hbufs[hb][:, :], pp[:, :], AF.Gelu,
                                                                           bias=b1c[:, m:m + 1])], "ac")
                        if m > 0:
                            em_logits(m - 1)
                        if hook is not None and m == hook_m:
                            hook()
                    em_logits(7)
                    prewarm(AF.Exp)

                def em_router_fwdT(br):
                    mx = 56 if br == 0 else 40
                    B.group("vector", [PS["pD"], bf("b2eb")], [LG],
                            [lambda v: v.tensor_single_scalar(lg_sb[:, :], pD[0:8, :], b2eb[:, 0:1], ALU.add)], "vd")
                    B.group("tensor", [LG, bf("eyef")], [PS["pE"]],
                            [lambda t, ss=ss: t.transpose(pE_[:, 8 * ss:8 * (ss + 1)],
                                                          lg_sb[:, 128 * ss:128 * (ss + 1)],
                                                          eyef[0:8, 0:8]) for ss in range(4)], "tp")
                    B.group("vector", [PS["pE"]], [bf("smx")],
                            [lambda v, ss=ss, mx=mx: v.reduce_max(sc[:, mx + ss:mx + ss + 1],
                                                                  pE_[:, 8 * ss:8 * (ss + 1)],
                                                                  axis=AX.X, negate=True) for ss in range(4)], "vd")
                    for ss in range(4):
                        B.group("scalar", [PS["pE"], bf("smx")], [XT],
                                [lambda s, ss=ss, br=br, mx=mx: s.activation(
                                    xtv[:, 4 * br + ss, 0:8],
                                    pE_[:, 8 * ss:8 * (ss + 1)], AF.Exp, bias=sc[:, mx + ss:mx + ss + 1])], "ac")
                    if br == 0:
                        prewarm(AF.Gelu)

                def em_router_backT(br, tc):
                    sm, rc = (44, 48) if br == 0 else (32, 36)
                    emits = [lambda v, ss=ss, br=br, sm=sm: v.reduce_sum(
                        sc[:, sm + ss:sm + ss + 1], xtv[:, 4 * br + ss, 0:8], axis=AX.X) for ss in range(4)]
                    emits.append(lambda v, sm=sm, rc=rc: v.reciprocal(sc[:, rc:rc + 4], sc[:, sm:sm + 4]))
                    emits += [lambda v, ss=ss, br=br, tc=tc, rc=rc: v.tensor_scalar(
                        wtm[:, 8 * ss:8 * (ss + 1)], xtv[:, 4 * br + ss, 0:8],
                        sc[:, rc + ss:rc + ss + 1], invn[:, 4 * tc + ss:4 * tc + ss + 1],
                        op0=ALU.mult, op1=ALU.mult) for ss in range(4)]
                    B.group("vector", [XT, bf("invn")], [WTM], emits, "vd")
                    B.group("tensor", [WTM, bf("eye")], [PS["pTb"]],
                            [lambda t, ss=ss: t.transpose(pTb[0:8, 128 * ss:128 * (ss + 1)],
                                                          wtm[:, 8 * ss:8 * (ss + 1)], eye[:, :])
                             for ss in range(4)], "tp")
                    B.group("vector", [PS["pTb"]], [WFMb[br]],
                            [lambda v, br=br: v.tensor_copy(wfm[:, 512 * br:512 * (br + 1)], pTb[0:8, 0:512])], "vd")

                em_router_pre(0)
                em_router_fwdT(0)
                em_router_pre(1, hook_m=3, hook=lambda tc=tc: em_router_backT(0, tc))
                em_router_fwdT(1)
                for br in range(2):
                    vsrc = dr["vbf_f"] if br == 0 else dr["vbf_i"]
                    wsrc = dr["wbfr_f"] if br == 0 else dr["wbfr_i"]
                    usrc = dr["ubf_f"] if br == 0 else dr["ubf_i"]
                    bmat, bmat_b = (bfwd_s, bf("bfwd_s")) if br == 0 else (binv_s, bf("binv_s"))
                    qsrc, qbuf = (qc_s, QC) if br == 0 else (iqc_s, IQC)
                    ksrc, kbuf = (kc_s, KC) if br == 0 else (ikc_s, IKC)
                    B.group("sync", [bf(_nm(vsrc))], [VWU[0]],
                            [lambda s, vsrc=vsrc: s.dma_start(
                                out=vwuS[:, 0:4096].rearrange("p (j c) -> p j c", j=8),
                                in_=vsrc.rearrange("(j p) c -> p j c", p=128))], "svS", 16)
                    B.group("sync", [bf(_nm(wsrc))], [VWU[1]],
                            [lambda s, wsrc=wsrc: s.dma_start(
                                out=vwuS[:, 4096:8192].rearrange("p (j c) -> p j c", j=8),
                                in_=wsrc.rearrange("(j p) c -> p j c", p=128))], "swS", 16)
                    B.group("sync", [bf(_nm(usrc))], [VWU[2]],
                            [lambda s, c=c, usrc=usrc: s.dma_start(
                                out=vwuS[:, 8192 + 1024 * c:8192 + 1024 * (c + 1)],
                                in_=usrc[:, 128 * c:128 * (c + 1)], transpose=True) for c in range(4)],
                            "suS", 16 * 4)
                    for c in range(4):
                        B.group("tensor", [VWU[1], kbuf], [PS["pB"]],
                                [lambda t, j=j, c=c, ksrc=ksrc: t.matmul(
                                    pB[:, :], vwuS[:, 4096 + 512 * j + 128 * c:4096 + 512 * j + 128 * (c + 1)],
                                    ksrc.rearrange("p (d t) -> p d t", t=512)[:, j, :],
                                    start=(j == 0), stop=(j == 7)) for j in range(8)], "tp")
                        if br == 1 and c == 0 and tc + 1 < TC:
                            em_rebuild_tp(tc + 1, 3)
                        B.group("tensor", [VWU[0], qbuf], [PS["pA"]],
                                [lambda t, j=j, c=c, qsrc=qsrc: t.matmul(
                                    pA[:, :], vwuS[:, 512 * j + 128 * c:512 * j + 128 * (c + 1)],
                                    qsrc.rearrange("p (d t) -> p d t", t=512)[:, j, :],
                                    start=(j == 0), stop=(j == 7)) for j in range(8)], "tp")
                        if br == 0 and c == 1:
                            em_router_backT(1, tc)
                        B.group("tensor", [WFMb[br], bf("selE")], [PS["pC"]],
                                [lambda t, c=c, br=br: t.matmul(pC[:, :], selE[:, 128 * c:128 * (c + 1)],
                                                                wfm[:, 512 * br:512 * (br + 1)],
                                                                start=True, stop=True)], "tp")
                        B.group("vector", [PS["pB"], PFX, bf("maskW")], [CUM],
                                [lambda v, br=br, c=c: v.tensor_tensor_scan(
                                    cum_s[:, :], pB[:, :], maskW[:, 0:512],
                                    prefix[:, 4 * br + c:4 * br + c + 1], op0=ALU.add, op1=ALU.bypass)], "vd")
                        B.group("vector", [CUM], [PFX],
                                [lambda v, br=br, c=c: v.tensor_copy(prefix[:, 4 * br + c:4 * br + c + 1],
                                                                     cum_s[:, 511:512])], "vd")
                        B.group("vector", [PS["pA"], CUM], [PTMP],
                                [lambda v: v.tensor_tensor(ptmp[:, :], pA[:, :], cum_s[:, :], ALU.mult)], "vd")
                        B.group("vector", [PTMP, PS["pC"]], [PROD],
                                [lambda v, c=c: v.tensor_tensor(prod_s[:, 512 * c:512 * (c + 1)],
                                                                ptmp[:, :], pC[:, :], ALU.mult)], "vd")
                        if br == 0 and tc + 1 < TC and c < 3:
                            if c < 2:
                                em_rebuild_dma(tc + 1, c)
                            else:
                                em_rebuild_id(tc + 1, 0)
                                em_rebuild_dma(tc + 1, 2)
                    for m in range(8):
                        if br == 0 and tc + 1 < TC:
                            if m in (0, 3, 6):
                                em_rebuild_tp(tc + 1, m // 3)
                                em_rebuild_id(tc + 1, m // 3 + 1)
                                if m == 0:
                                    em_rebuild_dma(tc + 1, 3)
                        pU, pUB = [(pD, PS["pD"]), (pC, PS["pC"]), (pO, PS["pO"])][m % 3]
                        emits = [lambda t, c=c, m=m, pU=pU: t.matmul(
                            pU[:, :], vwuS[:, 8192 + 1024 * c + 128 * m:8192 + 1024 * c + 128 * (m + 1)],
                            prod_s[:, 512 * c:512 * (c + 1)], start=(c == 0), stop=False) for c in range(4)]
                        emits.append(lambda t, m=m, br=br, bmat=bmat, pU=pU: t.matmul(
                            pU[:, :], bmat[:, 128 * m:128 * (m + 1)], wfm[:, 512 * br:512 * (br + 1)],
                            start=False, stop=True))
                        B.group("tensor", [VWU[2], PROD, WFMb[br], bmat_b], [pUB], emits, "tp")
                        if br == 0:
                            B.group("scalar", [pUB], [FFO],
                                    [lambda s2, m=m, tc=tc, pU=pU: s2.copy(
                                        fr[:, m, 512 * tc:512 * (tc + 1)], pU[:, :])], "ac")
                        else:
                            B.group("vector", [pUB, bf("cols")], [FFO],
                                    [lambda v, m=m, tc=tc, pU=pU: v.scalar_tensor_tensor(
                                        fr[:, m, 512 * tc:512 * (tc + 1)], pU[:, :],
                                        cols[:, CBI:CBI + 1], fr[:, m, 512 * tc:512 * (tc + 1)],
                                        op0=ALU.mult, op1=ALU.add)], "vd")
            if DBG and s_i == 0:
                B.group("sync", [FFO], [], [lambda s: s.dma_start(out=dbg_o["d_ffn"][:, :], in_=ffnout[:, :])], "ds", 16)
            owT, woT = wp[0], wp[1]
            OWT, WOT = WP[0], WP[1]
            B.group("sync", [bf("wbf_outw")], [OWT],
                    [lambda s, j=j: s.dma_start(out=owT[:, 1024 * j:1024 * (j + 1)],
                                                in_=wbf_outw[:, 128 * j:128 * (j + 1)], transpose=True)
                     for j in range(8)], "sow", 16 * 8)
            B.group("sync", [bf("wbf_wo")], [WOT],
                    [lambda s, j=j: s.dma_start(out=woT[:, 1024 * j:1024 * (j + 1)],
                                                in_=wbf_wo[:, 128 * j:128 * (j + 1)], transpose=True)
                     for j in range(8)], "swo", 16 * 8)

            st_fin = contextlib.ExitStack()
            out_f = [xsq, sb(st_fin, f"of{s_i}1", [128, 1024], F32, side="right")]
            OFb = [bf("xsq"), Buf()]
            for jc in range(NI):
                p = jc % 2
                B.group("sync", [XBb[p]], [XBb[p]],
                        [lambda s, s_i=s_i, jc=jc, p=p: s.dma_start(
                            out=xb[p][:, :], in_=x_in[s_i, 128 * jc:128 * (jc + 1), :])], f"sxb{p}", 16)
                if p == 0:
                    ptgt = [(pE_, PS["pE"], 0), (pE_, PS["pE"], 512)]
                else:
                    ptgt = [(pD, PS["pD"], 0), (pO, PS["pO"], 0)]
                for hh in range(2):
                    pt, ptB, off = ptgt[hh]
                    emits = []
                    for dd in range(8):
                        emits.append(lambda t, dd=dd, jc=jc, hh=hh, pt=pt, off=off: t.matmul(
                            pt[:, off:off + 512], ar[:, dd, 128 * jc:128 * (jc + 1)],
                            owT[:, 1024 * dd + 512 * hh:1024 * dd + 512 * (hh + 1)],
                            start=(dd == 0), stop=False))
                    for dd in range(8):
                        emits.append(lambda t, dd=dd, jc=jc, hh=hh, pt=pt, off=off: t.matmul(
                            pt[:, off:off + 512], fr[:, dd, 128 * jc:128 * (jc + 1)],
                            woT[:, 1024 * dd + 512 * hh:1024 * dd + 512 * (hh + 1)],
                            start=False, stop=(dd == 7)))
                    B.group("tensor", [ATT, FFO, OWT, WOT], [ptB], emits, "tp")
                for hh in range(2):
                    pt, ptB, off = ptgt[hh]
                    B.group("vector", [ptB, XBb[p]], [OFb[p]],
                            [lambda v, p=p, hh=hh, pt=pt, off=off: v.tensor_tensor(
                                out_f[p][:, 512 * hh:512 * (hh + 1)], pt[:, off:off + 512],
                                xb[p][:, 512 * hh:512 * (hh + 1)], ALU.add)], "vd")
                B.group("sync", [OFb[p]], [bf("yout")],
                        [lambda s, s_i=s_i, jc=jc, p=p: s.dma_start(
                            out=y_out[s_i, 128 * jc:128 * (jc + 1), :], in_=out_f[p][:, :])], f"sst{p}", 16)
            B.fence()
            st_fin.close()
            st_ffn.close()
            st_out.close()
            B.fence(force=True)

        for s_nm, rec in sems.items():
            if s_nm in ("tp", "ac", "vd") or rec[1] == 0:
                continue
            B.emit("sync", lambda s, h=rec[0], v=rec[1]: s.wait_ge(h, v))
        B.emit("gpsimd", lambda g, h=sems["dg"][0], v=sems["dg"][1]: g.wait_ge(h, v))

        blk = E(nc.Block())
        blk.tensor(lambda e: [f(e) for f in B.ops["tensor"]])
        blk.scalar(lambda e: [f(e) for f in B.ops["scalar"]])
        blk.vector(lambda e: [f(e) for f in B.ops["vector"]])
        blk.sync(lambda e: [f(e) for f in B.ops["sync"]])
        blk.gpsimd(lambda e: [f(e) for f in B.ops["gpsimd"]])
    return nc


def host_consts(T=2048):
    bft = ml_dtypes.bfloat16
    c = {}
    c["c_eye"] = np.eye(128, dtype=bft)
    c["c_eyef"] = np.eye(128, dtype=np.float32)
    c["c_negI"] = (np.eye(128) * -30000.0).astype(bft)
    j = np.arange(128)[:, None]
    u = np.arange(896)[None, :]
    c["c_maskW"] = (u >= j + 384).astype(bft)
    c["c_eye64"] = np.eye(64, dtype=bft)
    e6 = np.zeros((6, 12), dtype=bft); e6[np.arange(6), np.arange(6)] = 1
    c["c_eyeT6"] = e6
    e6b = np.zeros((6, 12), dtype=bft); e6b[np.arange(6), 6 + np.arange(6)] = 1
    c["c_eyeB6"] = e6b
    sel = np.zeros((8, 512), dtype=bft)
    for cc in range(4):
        for m in range(128):
            sel[2 * cc + m // 64, 128 * cc + m] = 1
    c["c_selE"] = sel
    tte = np.zeros((12, 12 * 16), dtype=bft)
    for h in range(16):
        for dd in range(6):
            tte[dd, 12 * h + dd] = 1
    c["c_TTeye"] = tte
    c["c_ones1"] = np.ones((1, 128), dtype=np.float32)
    t_idx = np.arange(T).reshape(T // 128, 128).T
    c["c_invn"] = (1.0 / (t_idx + 1)).astype(np.float32)
    return c


def make_in_maps(x, weights, T=2048, SEQ=2, cores=NCORES):
    shared = dict(weights)
    shared.update(host_consts(T=T))
    in_maps = []
    for core in range(cores):
        m = dict(shared)
        m["x"] = np.ascontiguousarray(x[SEQ * core:SEQ * (core + 1), :T])
        in_maps.append(m)
    return in_maps


def pack_weights(kw):
    f = lambda a: np.ascontiguousarray(np.asarray(a, dtype=np.float32))
    return dict(
        qkv_w=f(kw["L0_attn_qkv_w"]), out_w=f(kw["L0_attn_out_w"]), hex_w=f(kw["L0_attn_hex_w"]),
        lam_logit=f(kw["L0_attn_hamming_lambda_logit"]).reshape(1), q6_w=f(kw["L0_attn_q6_w"]),
        transforms=f(kw["L0_attn_transforms"]), tw=f(kw["L0_attn_transform_weights"]),
        scale_logit=f(kw["L0_attn_scale_logit"]).reshape(1),
        sips_alpha=f(kw["L0_attn_sips_alpha"]).reshape(1),
        sips_beta=f(kw["L0_attn_sips_beta"]).reshape(1),
        w_q=f(kw["L0_ffn_W_Q"]), w_k=f(kw["L0_ffn_W_K"]), w_o=f(kw["L0_ffn_W_O"]),
        w_inv=f(kw["L0_ffn_W_inv"]), v_fwd=f(kw["L0_ffn_V_fwd"]), w_fwd=f(kw["L0_ffn_W_fwd"]),
        u_fwd=f(kw["L0_ffn_U_fwd"]), b_fwd=f(kw["L0_ffn_b_fwd"]), v_inv=f(kw["L0_ffn_V_inv"]),
        w_inv_exp=f(kw["L0_ffn_W_inv_exp"]), u_inv=f(kw["L0_ffn_U_inv"]), b_inv=f(kw["L0_ffn_b_inv"]),
        rw1=f(kw["L0_ffn_router_w1"]), rb1=f(kw["L0_ffn_router_b1"]), rw2=f(kw["L0_ffn_router_w2"]),
        rb2=f(kw["L0_ffn_router_b2"]), alpha_bi=f(kw["L0_ffn_alpha_bi"]).reshape(1),
        expert_bias=f(kw["L0_ffn_expert_bias"]),
    )


def kernel(**kw):
    T, SEQ = 2048, 2
    nc = build_program(T=T, SEQ=SEQ)
    x = np.ascontiguousarray(np.asarray(kw["L0_attn_x"], dtype=np.float32))
    in_maps = make_in_maps(x, pack_weights(kw), T=T, SEQ=SEQ)
    res = run_bass_kernel_spmd(nc, in_maps, core_ids=list(range(NCORES)))
    out = np.empty((NCORES * SEQ, T, D), dtype=np.float32)
    for core in range(NCORES):
        out[SEQ * core:SEQ * (core + 1)] = np.asarray(res.results[core]["y"])
    return out


# version 27 (final)
# speedup vs baseline: 1.0180x; 1.0180x over previous
import contextlib
import numpy as np
import ml_dtypes
import concourse.bass as bass
import concourse.mybir as mybir
from concourse.bass_utils import run_bass_kernel_spmd

F32 = mybir.dt.float32
BF16 = mybir.dt.bfloat16
AF = mybir.ActivationFunctionType
ALU = mybir.AluOpType
AX = mybir.AxisListType

D = 1024
H = 16
KEXP = 8
RR = 64
KR = KEXP * RR
NCORES = 8
ENGINES = ("tensor", "scalar", "vector", "sync", "gpsimd")



def _nm(x):
    t = getattr(x, "tensor", None)
    return t.name if t is not None else x.name

class Buf:
    __slots__ = ("w", "readers")
    def __init__(self):
        self.w = None
        self.readers = []


class Builder:
    def __init__(self, nc, sems):
        self.nc = nc
        self.sems = sems
        self.ops = {e: [] for e in ENGINES}
        self.memo = {e: {} for e in ENGINES}

    def wait_tok(self, eng, tok, force=False):
        if tok is None:
            return
        s, v, prod = tok
        if not force:
            if prod == eng and s in ("tp", "gp"):
                return
            if self.memo[eng].get(s, -1) >= v:
                return
        if self.memo[eng].get(s, -1) < v:
            self.memo[eng][s] = v
        h = self.sems[s][0]
        self.ops[eng].append(lambda e, h=h, v=v: e.wait_ge(h, v))

    def emit(self, eng, fn, sem=None, n=1):
        if sem is None:
            self.ops[eng].append(fn)
            return None
        rec = self.sems[sem]
        rec[1] += n
        h = rec[0]
        self.ops[eng].append(lambda e, fn=fn, h=h, n=n: fn(e).then_inc(h, n))
        return (sem, rec[1], eng)

    def group(self, eng, reads, writes, emits, sem, n=1):
        for b in reads:
            if b is not None:
                self.wait_tok(eng, b.w)
        for b in writes:
            if b is None:
                continue
            self.wait_tok(eng, b.w)
            for t in b.readers:
                self.wait_tok(eng, t)
        tok = None
        if sem not in ("tp", "ac", "vd", "gp"):
            for fn in emits:
                tok = self.emit(eng, fn, sem=sem, n=16)
        elif sem in ("ac", "vd", "gp") and len(emits) > 1:
            h = self.sems[sem][0]
            for i, fn in enumerate(emits):
                if i > 0:
                    v = self.sems[sem][1]
                    self.ops[eng].append(lambda e, h=h, v=v: e.wait_ge(h, v))
                tok = self.emit(eng, fn, sem=sem, n=1)
        else:
            for i, fn in enumerate(emits):
                tok = self.emit(eng, fn, sem=sem if i == len(emits) - 1 else None, n=n)
        for b in reads:
            if b is not None:
                b.readers.append(tok)
        for b in writes:
            if b is not None:
                b.w = tok
                b.readers = []
        return tok

    def cur_toks(self):
        toks = [("tp", self.sems["tp"][1], "tensor"),
                ("ac", self.sems["ac"][1], "scalar"),
                ("vd", self.sems["vd"][1], "vector")]
        for s, rec in self.sems.items():
            if s not in ("tp", "ac", "vd") and rec[1] > 0:
                toks.append((s, rec[1], "sync"))
        return toks

    def fence(self, engines=ENGINES, force=False):
        toks = self.cur_toks()
        for e in engines:
            for t in toks:
                self.wait_tok(e, t, force=force)


DBG = False

def build_program(T=2048, SEQ=2):
    nc = bass.Bass("TRN2", target_bir_lowering=False)
    TC = T // 512
    NI = T // 128
    NICH = T // 512

    def di(name, shape, dt=F32):
        return nc.dram_tensor(name, list(shape), dt, kind="ExternalInput").ap()

    x_in = di("x", [SEQ, T, D])
    qkv_w = di("qkv_w", [3 * D, D]); out_w = di("out_w", [D, D])
    hex_w = di("hex_w", [6, D]); lam_logit = di("lam_logit", [1])
    q6_w = di("q6_w", [6, D]); transforms = di("transforms", [7, 6, 6])
    tw_in = di("tw", [H, 7]); scale_logit = di("scale_logit", [1])
    sips_alpha = di("sips_alpha", [1]); sips_beta = di("sips_beta", [1])
    w_q = di("w_q", [D, D]); w_k = di("w_k", [D, D]); w_o = di("w_o", [D, D])
    w_inv = di("w_inv", [D, D])
    v_fwd = di("v_fwd", [KEXP, D, RR]); w_fwd = di("w_fwd", [KEXP, D, RR])
    u_fwd = di("u_fwd", [KEXP, D, RR]); b_fwd = di("b_fwd", [KEXP, D])
    v_inv = di("v_inv", [KEXP, D, RR]); w_inv_exp = di("w_inv_exp", [KEXP, D, RR])
    u_inv = di("u_inv", [KEXP, D, RR]); b_inv = di("b_inv", [KEXP, D])
    rw1 = di("rw1", [D, D]); rb1 = di("rb1", [D]); rw2 = di("rw2", [KEXP, D])
    rb2 = di("rb2", [KEXP]); alpha_bi = di("alpha_bi", [1]); expert_bias = di("expert_bias", [KEXP])
    c_eye = di("c_eye", [128, 128], BF16)
    c_eyef = di("c_eyef", [128, 128], F32)
    c_negI = di("c_negI", [128, 128], BF16)
    c_maskW = di("c_maskW", [128, 896], BF16)
    c_eye64 = di("c_eye64", [64, 64], BF16)
    c_eyeT6 = di("c_eyeT6", [6, 12], BF16)
    c_eyeB6 = di("c_eyeB6", [6, 12], BF16)
    c_selE = di("c_selE", [8, 512], BF16)
    c_TTeye = di("c_TTeye", [12, 12 * H], BF16)
    c_ones1 = di("c_ones1", [1, 128], F32)
    c_invn = di("c_invn", [128, NI], F32)

    y_out = nc.dram_tensor("y", [SEQ, T, D], F32, kind="ExternalOutput").ap()
    dbg_o = {}
    if DBG:
        for nm_, sh in (("d_xn", [128, 8 * T]), ("d_q", [128, 8 * T]), ("d_k", [128, 8 * T]),
                        ("d_att", [128, 8 * T]), ("d_ffn", [128, 8 * T]), ("d_vext", [128, (T // 128) * 16 * 65]),
                        ("d_lhs0", [12, T]), ("d_lhsL", [12, T]), ("d_crh0", [12, T]), ("d_lsl", [64, 512])):
            dbg_o[nm_] = nc.dram_tensor(nm_, sh, BF16 if nm_ != "d_lsl" else F32, kind="ExternalOutput").ap()
        dbg_o["d_stats"] = nc.dram_tensor("d_stats", [128, 2 * (T // 128)], F32, kind="ExternalOutput").ap()
        dbg_o["d_xt"] = nc.dram_tensor("d_xt", [128, 1024], BF16, kind="ExternalOutput").ap()
        dbg_o["d_var"] = nc.dram_tensor("d_var", [128, 2 * (T // 128)], F32, kind="ExternalOutput").ap()
        dbg_o["d_xb0"] = nc.dram_tensor("d_xb0", [128, 1024], F32, kind="ExternalOutput").ap()
        dbg_o["d_xsq"] = nc.dram_tensor("d_xsq", [128, 1024], F32, kind="ExternalOutput").ap()
        dbg_o["d_ptb"] = nc.dram_tensor("d_ptb", [128, 1024], F32, kind="ExternalOutput").ap()

    dr = {}
    def dram(name, shape):
        dr[name] = nc.dram_tensor(name, list(shape), BF16, kind="Internal").ap()
        return dr[name]
    wbf_qkv = dram("wbf_qkv", [3 * D, D])
    wbf_outw = dram("wbf_outw", [D, D]); wbf_wq = dram("wbf_wq", [D, D])
    wbf_wk = dram("wbf_wk", [D, D]); wbf_wo = dram("wbf_wo", [D, D])
    wbf_winv = dram("wbf_winv", [D, D]); wbf_rw1 = dram("wbf_rw1", [D, D])
    vwu_src = {"vbf_f": v_fwd, "wbfr_f": w_fwd, "ubf_f": u_fwd,
               "vbf_i": w_inv_exp, "wbfr_i": v_inv, "ubf_i": u_inv}
    for nm in vwu_src:
        dram(nm, [D, KR])
    mh_dram = dram("mh_dram", [16, 36])

    stk = contextlib.ExitStack()
    with stk:
        E = stk.enter_context
        sems = {s: [E(nc.semaphore(s)), 0] for s in ("ds", "dg", "tp", "ac", "vd", "sxb0", "sxb1", "sxb2", "sxb3", "sqT", "sqT1", "svT", "swp0", "swp1", "svS", "swS", "suS", "sow", "swo", "stl0", "stl1", "sst0", "sst1", "gp", "dgq")}
        B = Builder(nc, sems)
        bufs = {}
        def bf(key):
            if key not in bufs:
                bufs[key] = Buf()
            return bufs[key]

        def sb(st_, name, shape, dt=F32, side="left"):
            return st_.enter_context(nc.sbuf_tensor(name, list(shape), dt, side=side))

        eye = sb(stk, "eye", [128, 128], BF16)
        eyef = sb(stk, "eyef", [128, 128], F32)
        negI = sb(stk, "negI", [128, 128], BF16)
        maskW = sb(stk, "maskW", [128, 896], BF16)
        eye64 = sb(stk, "eye64", [64, 64], BF16)
        eyeT6 = sb(stk, "eyeT6", [6, 12], BF16)
        eyeB6 = sb(stk, "eyeB6", [6, 12], BF16)
        selE = sb(stk, "selE", [8, 512], BF16)
        TTall = sb(stk, "TTall", [12, 12 * H], BF16)
        ones1 = sb(stk, "ones1", [1, 128], F32)
        invn = sb(stk, "invn", [128, NI], F32)
        b1c = sb(stk, "b1c", [128, 8], F32)
        b2eb = sb(stk, "b2eb", [8, 1], F32)
        rw2T = sb(stk, "rw2T", [128, 64], BF16)
        hexT = sb(stk, "hexT", [128, 48], BF16)
        q6T = sb(stk, "q6T", [128, 48], BF16)
        bfwd_s = sb(stk, "bfwd_s", [8, D], BF16)
        binv_s = sb(stk, "binv_s", [8, D], BF16)
        aI = sb(stk, "aI", [128, 128], BF16)
        stats2 = [sb(stk, f"stats{p}", [128, 2 * NI], F32) for p in range(2)]
        mrs2 = [sb(stk, f"mrs{p}", [128, NI], F32) for p in range(2)]
        l_slab = sb(stk, "l_slab", [64, 512], F32)
        rinv_s = sb(stk, "rinv_s", [64, 512], F32)
        rhl = sb(stk, "rhl", [64, 1024], BF16)
        sc = sb(stk, "sc", [128, 64], F32)
        tr736 = sb(stk, "tr736", [7, 36], F32)
        tr736s = sb(stk, "tr736s", [7, 36], F32)
        tr252 = sb(stk, "tr252", [1, 256], F32)
        tws = sb(stk, "tws", [16, 8], F32)
        mh_bf = sb(stk, "mh_bf", [16, 36], BF16)
        cols = sb(stk, "cols", [128, 8], F32)
        col6 = sb(stk, "col6", [6, 1], F32)
        CIB, CNM, CAL, CBI, CEP = 0, 1, 2, 3, 4

        pA = E(nc.psum_tensor("pA", [128, 512], F32))
        pB = E(nc.psum_tensor("pB", [128, 512], F32))
        pC = E(nc.psum_tensor("pC", [128, 512], F32))
        pD = E(nc.psum_tensor("pD", [128, 512], F32))
        pO = E(nc.psum_tensor("pO", [128, 512], F32))
        pE_ = E(nc.psum_tensor("pE", [128, 1024], F32))
        pTb = E(nc.psum_tensor("pTb", [128, 1024], BF16))
        PS = {k: bf(k) for k in ("pA", "pB", "pC", "pD", "pO", "pE", "pTb")}

        B.group("gpsimd", [], [bf(_nm(wbf_qkv))],
                [lambda g: g.dma_start(out=wbf_qkv[:, :], in_=qkv_w[:, :])], "dgq", 16)
        def em_big_casts(gate_tok):
            B.wait_tok("gpsimd", gate_tok, force=True)
            B.group("gpsimd", [], [bf(_nm(wbf_outw))],
                    [lambda g: g.dma_start(out=wbf_outw[:, :], in_=out_w[:, :])], "dg", 16)
            for dst, src in ((wbf_wq, w_q), (wbf_wk, w_k),
                             (wbf_wo, w_o), (wbf_winv, w_inv), (wbf_rw1, rw1)):
                B.group("gpsimd", [], [bf(_nm(dst))],
                        [lambda g, dst=dst, src=src: g.dma_start(out=dst[:, :], in_=src[:, :])], "dg", 16)
            for nm, src in vwu_src.items():
                dst = dr[nm]
                B.group("gpsimd", [], [bf(nm)],
                        [lambda g, dst=dst, src=src, k=k:
                         g.dma_start(out=dst[:, RR * k:RR * (k + 1)], in_=src[k, :, :]) for k in range(KEXP)],
                        "dg", 16 * KEXP)
            B.group("gpsimd", [], [bf("bfwd_s")], [lambda g: g.dma_start(out=bfwd_s[:, :], in_=b_fwd[:, :])], "dg", 16)
            B.group("gpsimd", [], [bf("binv_s")], [lambda g: g.dma_start(out=binv_s[:, :], in_=b_inv[:, :])], "dg", 16)
            _tok = ("dg", B.sems["dg"][1], "gpsimd")
            for _b in bufs.values():
                if _b.w is not None and _b.w[0] == "dg":
                    _b.w = _tok

        st_init = contextlib.ExitStack()
        rw2bf = sb(st_init, "rw2bf", [8, D], BF16, side="right")
        hexbf = sb(st_init, "hexbf", [6, D], BF16, side="right")
        q6bf = sb(st_init, "q6bf", [6, D], BF16, side="right")
        B.group("gpsimd", [], [bf("rw2bf")], [lambda g: g.dma_start(out=rw2bf[:, :], in_=rw2[:, :])], "dg", 16)
        B.group("gpsimd", [], [bf("hexbf")], [lambda g: g.dma_start(out=hexbf[:, :], in_=hex_w[:, :])], "dg", 16)
        B.group("gpsimd", [], [bf("q6bf")], [lambda g: g.dma_start(out=q6bf[:, :], in_=q6_w[:, :])], "dg", 16)

        for dst, src in ((eye, c_eye), (eyef, c_eyef), (negI, c_negI), (maskW, c_maskW),
                         (eye64, c_eye64), (eyeT6, c_eyeT6), (eyeB6, c_eyeB6), (selE, c_selE),
                         (TTall, c_TTeye), (ones1, c_ones1), (invn, c_invn)):
            B.group("sync", [], [bf(_nm(dst))],
                    [lambda s, dst=dst, src=src: s.dma_start(out=dst[:, :], in_=src[:, :])], "ds", 16)
        B.group("sync", [], [bf("b1c")],
                [lambda s: s.dma_start(out=b1c[:, :], in_=rb1.rearrange("(m p) -> p m", p=128), allow_slow_non_contiguous=True)], "ds", 16)
        B.group("sync", [], [bf("sc_l")], [
            lambda s: s.dma_start(out=sc[0:1, 0:1], in_=lam_logit.rearrange("(a b) -> a b", a=1)),
            lambda s: s.dma_start(out=sc[0:1, 1:2], in_=scale_logit.rearrange("(a b) -> a b", a=1)),
            lambda s: s.dma_start(out=sc[0:1, 2:3], in_=sips_alpha.rearrange("(a b) -> a b", a=1)),
            lambda s: s.dma_start(out=sc[0:1, 3:4], in_=sips_beta.rearrange("(a b) -> a b", a=1)),
            lambda s: s.dma_start(out=sc[0:1, 4:5], in_=alpha_bi.rearrange("(a b) -> a b", a=1)),
            lambda s: s.dma_start(out=sc[0:8, 5:6], in_=rb2.rearrange("(k o) -> k o", o=1)),
            lambda s: s.dma_start(out=sc[0:8, 6:7], in_=expert_bias.rearrange("(k o) -> k o", o=1)),
        ], "ds", 16 * 7)
        B.group("sync", [], [bf("tr736")],
                [lambda s: s.dma_start(out=tr736[:, :], in_=transforms.rearrange("t d e -> t (d e)"))], "ds", 16)
        B.group("sync", [], [bf("tr252")],
                [lambda s: s.dma_start(out=tr252[0:1, 0:252],
                                       in_=transforms.rearrange("t d e -> (t d e)").rearrange("(a b) -> a b", a=1))],
                "ds", 16)
        B.group("sync", [], [bf("tws")],
                [lambda s: s.dma_start(out=tws[:, 0:7], in_=tw_in[:, :])], "ds", 16)
        _ds_tok = ("ds", B.sems["ds"][1], "sync")
        _dg_tok = ("dg", B.sems["dg"][1], "gpsimd")
        for _k, _b in list(bufs.items()):
            if _b.w is not None and _b.w[0] == "ds":
                _b.w = _ds_tok
            if _b.w is not None and _b.w[0] == "dg":
                _b.w = _dg_tok

        B.group("scalar", [bf("sc_l")], [bf("sc_sig")], [
            lambda s: s.activation(sc[0:1, 8:9], sc[0:1, 0:1], AF.Sigmoid),
            lambda s: s.activation(sc[0:1, 9:10], sc[0:1, 1:2], AF.Sigmoid),
        ], "ac")
        B.group("vector", [bf("sc_sig"), bf("tr252"), bf("sc_l")], [bf("sc_d"), bf("b2eb")], [
            lambda v: v.tensor_single_scalar(sc[0:1, 10:11], sc[0:1, 8:9], 0.5, ALU.mult),
            lambda v: v.tensor_single_scalar(sc[0:1, 16:17], sc[0:1, 9:10], 2.0, ALU.mult),
            lambda v: v.tensor_single_scalar(sc[0:1, 11:12], sc[0:1, 16:17], 1.0 / 6.0, ALU.mult),
            lambda v: v.reciprocal(sc[0:1, 12:13], sc[0:1, 3:4]),
            lambda v: v.scalar_tensor_tensor(sc[0:1, 13:14], sc[0:1, 2:3], -1.0, sc[0:1, 2:3], op0=ALU.mult, op1=ALU.max),
            lambda v: v.reduce_max(sc[0:1, 14:15], tr252[0:1, 0:252], axis=AX.X, apply_absolute_value=True),
            lambda v: v.tensor_tensor(sc[0:1, 17:18], sc[0:1, 14:15], sc[0:1, 16:17], ALU.mult),
            lambda v: v.tensor_single_scalar(sc[0:1, 17:18], sc[0:1, 17:18], 6.0, ALU.mult),
            lambda v: v.tensor_single_scalar(sc[0:1, 18:19], sc[0:1, 8:9], 3.0, ALU.mult),
            lambda v: v.tensor_tensor(sc[0:1, 15:16], sc[0:1, 13:14], sc[0:1, 18:19], ALU.add),
            lambda v: v.tensor_tensor(sc[0:1, 15:16], sc[0:1, 15:16], sc[0:1, 17:18], ALU.add),
            lambda v: v.tensor_scalar(sc[0:1, 15:16], sc[0:1, 15:16], -1.0, 43.0, op0=ALU.mult, op1=ALU.add),
            lambda v: v.tensor_tensor(b2eb[:, 0:1], sc[0:8, 5:6], sc[0:8, 6:7], ALU.add),
        ], "vd")

        def bcast(src_ap, dst_ap, nrow, dstbuf):
            B.group("tensor", [bf("sc_d"), bf("ones1")], [PS["pE"]],
                    [lambda t, s=src_ap, n=nrow: t.matmul(pE_[0:n, 0:1], ones1[0:1, 0:n], s,
                                                          start=True, stop=True)], "tp")
            B.group("vector", [PS["pE"]], [dstbuf],
                    [lambda v, d=dst_ap, n=nrow: v.tensor_copy(d, pE_[0:n, 0:1])], "vd")
        bcast(sc[0:1, 12:13], cols[:, CIB:CIB + 1], 128, bf("cols"))
        bcast(sc[0:1, 15:16], cols[:, CNM:CNM + 1], 128, bf("cols"))
        bcast(sc[0:1, 2:3], cols[:, CAL:CAL + 1], 128, bf("cols"))
        bcast(sc[0:1, 4:5], cols[:, CBI:CBI + 1], 128, bf("cols"))
        bcast(sc[0:1, 10:11], col6[0:6, 0:1], 6, bf("col6"))
        bcast(sc[0:1, 11:12], tws[0:7, 7:8], 7, bf("sg26"))
        B.group("vector", [], [bf("cols")],
                [lambda v: v.memset(cols[:, CEP:CEP + 1], 1e-5)], "vd")
        B.group("vector", [], [],
                [lambda v: v.memset(sc[0:1, 63:64], 0.0)], "vd")
        B.group("vector", [bf("eye"), bf("cols")], [bf("aI")],
                [lambda v: v.tensor_scalar_mul(aI[:, :], eye[:, :], cols[:, CAL:CAL + 1])], "vd")
        B.group("vector", [bf("tws")], [bf("tws_m")],
                [lambda v: v.reduce_max(sc[0:16, 20:21], tws[0:16, 0:7], axis=AX.X, negate=True)], "vd")
        B.group("scalar", [bf("tws_m")], [bf("tws_e")],
                [lambda s: s.activation(sc[0:16, 24:31], tws[0:16, 0:7], AF.Exp, bias=sc[0:16, 20:21])], "ac")
        B.group("vector", [bf("tws_e")], [bf("tws_w")], [
            lambda v: v.reduce_sum(sc[0:16, 21:22], sc[0:16, 24:31], axis=AX.X),
            lambda v: v.reciprocal(sc[0:16, 22:23], sc[0:16, 21:22]),
            lambda v: v.tensor_scalar_mul(tws[0:16, 0:7], sc[0:16, 24:31], sc[0:16, 22:23]),
        ], "vd")
        B.group("vector", [bf("tr736"), bf("sg26")], [bf("tr736s")],
                [lambda v: v.tensor_scalar_mul(tr736s[:, :], tr736[:, :], tws[0:7, 7:8])], "vd")
        B.group("tensor", [bf("tws_w"), bf("eyef")], [PS["pE"]],
                [lambda t: t.transpose(pE_[0:7, 16:32], tws[0:16, 0:7], eyef[0:16, 0:16])], "tp")
        B.group("vector", [PS["pE"]], [bf("wT")],
                [lambda v: v.tensor_copy(sc[0:7, 32:48], pE_[0:7, 16:32])], "vd")
        B.group("tensor", [bf("wT"), bf("tr736s")], [PS["pE"]],
                [lambda t: t.matmul(pE_[0:16, 32:68], sc[0:7, 32:48], tr736s[0:7, 0:36],
                                    start=True, stop=True)], "tp")
        B.group("vector", [PS["pE"]], [bf("mh_bf")],
                [lambda v: v.tensor_copy(mh_bf[:, :], pE_[0:16, 32:68])], "vd")
        B.group("sync", [bf("mh_bf")], [bf("mh_dram")],
                [lambda s: s.dma_start(out=mh_dram[:, :], in_=mh_bf[:, :])], "ds", 16)
        B.group("sync", [bf("mh_dram"), bf("TTall")], [bf("TTall")],
                [lambda s, h=h: s.dma_start(out=TTall[6:12, 12 * h + 6:12 * h + 12],
                                            in_=mh_dram[h, :].rearrange("(d e) -> d e", d=6))
                 for h in range(H)], "ds", 16 * H)
        for m in range(8):
            B.group("tensor", [bf("rw2bf"), bf("eye")], [PS["pTb"]],
                    [lambda t, m=m: t.transpose(pTb[:, 8 * m:8 * (m + 1)],
                                                rw2bf[:, 128 * m:128 * (m + 1)], eye[0:8, 0:8])], "tp")
        B.group("vector", [PS["pTb"]], [bf("rw2T")],
                [lambda v: v.tensor_copy(rw2T[:, :], pTb[:, 0:64])], "vd")
        for src_s, dst_s, key in ((hexbf, hexT, "hexT"), (q6bf, q6T, "q6T")):
            for j in range(8):
                B.group("tensor", [bf(_nm(src_s)), bf("eye")], [PS["pTb"]],
                        [lambda t, src_s=src_s, j=j: t.transpose(pTb[:, 64 + 6 * j:64 + 6 * (j + 1)],
                                                                 src_s[:, 128 * j:128 * (j + 1)],
                                                                 eye[0:6, 0:6])], "tp")
            B.group("vector", [PS["pTb"]], [bf(key)],
                    [lambda v, dst_s=dst_s: v.tensor_copy(dst_s[:, :], pTb[:, 64:112])], "vd")
        B.fence()
        st_init.close()
        B.fence()

        xb = [sb(stk, f"xb{p}", [128, 1024], F32, side="right") for p in range(2)]
        xt_tm = sb(stk, "xt_tm", [128, 1024], BF16, side="right")
        xsq = sb(stk, "xsq", [128, 1024], F32, side="right")
        XBb = [bf("xb0"), bf("xb1")]
        XT = bf("xt_tm")

        xb2bufs = ((0, "sxb0"), (1, "sxb1"))

        def em_ln_chunk(s_j, st_t, ST_B, i, bufs=xb2bufs, xbl=None):
            bi, semn = bufs[i % len(bufs)]
            xbp = (xbl or xb)[bi]
            XBp = bf(_nm(xbp))
            B.group("sync", [XBp], [XBp],
                    [lambda s, s_j=s_j, i=i, xbp=xbp: s.dma_start(
                        out=xbp[:, :], in_=x_in[s_j, 128 * i:128 * (i + 1), :])], semn, 16)
            B.group("vector", [XBp], [bf("xsq"), ST_B], [
                lambda v, xbp=xbp: v.tensor_tensor(xsq[:, :], xbp[:, :], xbp[:, :], ALU.mult),
                lambda v, xbp=xbp: v.reduce_sum(sc[:, 49:50], xbp[:, :], axis=AX.X),
                lambda v: v.reduce_sum(sc[:, 50:51], xsq[:, :], axis=AX.X),
                lambda v, i=i, st_t=st_t: v.tensor_single_scalar(st_t[:, 2 * i:2 * i + 1], sc[:, 49:50], 1.0 / D, ALU.mult),
                lambda v: v.tensor_single_scalar(sc[:, 51:52], sc[:, 50:51], 1.0 / D, ALU.mult),
                lambda v, i=i, st_t=st_t: v.tensor_tensor(sc[:, 52:53], st_t[:, 2 * i:2 * i + 1],
                                               st_t[:, 2 * i:2 * i + 1], ALU.mult),
                lambda v, i=i, st_t=st_t: v.tensor_tensor(st_t[:, 2 * i + 1:2 * i + 2], sc[:, 51:52],
                                               sc[:, 52:53], ALU.subtract),
            ], "vd")

        for s_i in range(SEQ):
            st_seq = contextlib.ExitStack()
            stats = stats2[s_i % 2]
            mrs = mrs2[s_i % 2]
            STATS = bf(f"stats{s_i % 2}")
            q_slab = sb(st_seq, f"q_slab{s_i}", [128, 8 * T], BF16, side="right")
            k_slab = sb(st_seq, f"k_slab{s_i}", [128, 8 * T], BF16, side="right")
            v_ext = sb(st_seq, f"v_ext{s_i}", [128, NI * 16 * 65], BF16, side="right")
            lhs0_t = sb(st_seq, f"lhs0_t{s_i}", [12, T], BF16, side="right")
            lhsL_t = sb(st_seq, f"lhsL_t{s_i}", [128, T], BF16, side="right")
            QS, KS, VS = Buf(), Buf(), Buf()
            SBT, SBL, HXC, L0, LL = Buf(), Buf(), Buf(), Buf(), Buf()
            CRHb = [Buf(), Buf()]
            TSBb = [Buf(), Buf()]
            PSBb = [Buf(), Buf(), Buf()]
            SSBb = [Buf()]
            RB128, LSL = Buf(), Buf()
            TMPL = [Buf(), Buf()]
            B.group("vector", [], [LL], [lambda v: v.memset(lhsL_t[:, :], 0.0)], "vd")
            qr = q_slab.rearrange("p (m t) -> p m t", t=T)
            krr = k_slab.rearrange("p (m t) -> p m t", t=T)
            ver = v_ext.rearrange("p (jc h c) -> p jc h c", h=16, c=65)

            st_ln = contextlib.ExitStack()
            xn_slab = sb(st_ln, f"xn{s_i}", [128, 8 * T], BF16, side="right")
            sb_t = sb(st_ln, f"sb_t{s_i}", [6, T], BF16, side="right")
            sbL_t = sb(st_ln, f"sbL_t{s_i}", [6, T], BF16, side="right")
            hexc_t = sb(st_ln, f"hexc_t{s_i}", [6, T], BF16, side="right")
            XNS = Buf()
            xnr = xn_slab.rearrange("p (d t) -> p d t", t=T)

            def em_ln_norm(i, bufs=xb2bufs, xbl=None):
                bi, _semn = bufs[i % len(bufs)]
                xbp = (xbl or xb)[bi]
                B.group("scalar", [bf(_nm(xbp)), STATS, bf(f"mrs{s_i % 2}")], [XT],
                        [lambda s, i=i, xbp=xbp, stats=stats, mrs=mrs: s.activation(
                            xt_tm[:, :], xbp[:, :], AF.Identity,
                            bias=mrs[:, i:i + 1], scale=stats[:, 2 * i + 1:2 * i + 2])], "ac")
                B.group("tensor", [XT, bf("eye")], [PS["pTb"]],
                        [lambda t, d=d: t.transpose(pTb[:, 128 * d:128 * (d + 1)],
                                                    xt_tm[:, 128 * d:128 * (d + 1)], eye[:, :])
                         for d in range(8)], "tp")
                if DBG and s_i == 0 and i == 0:
                    B.group("sync", [XT], [], [lambda s: s.dma_start(out=dbg_o["d_xt"][:, :], in_=xt_tm[:, :])], "ds", 16)
                    B.group("vector", [PS["pTb"]], [bf("xsq")],
                            [lambda v: v.tensor_copy(xsq[:, :], pTb[:, :])], "vd")
                    B.group("sync", [bf("xsq")], [], [lambda s: s.dma_start(out=dbg_o["d_ptb"][:, :], in_=xsq[:, :])], "ds", 16)
                B.group("vector", [PS["pTb"]], [XNS],
                        [lambda v, i=i: v.tensor_copy(xnr[:, :, 128 * i:128 * (i + 1)],
                                                      pTb.rearrange("p (d t) -> p d t", t=128))], "vd")

            st_qk = contextlib.ExitStack()
            qTs = [sb(st_qk, f"qT{s_i}", [128, 8192], BF16, side="right"),
                   v_ext[:, 0:8192] if NI * 16 * 65 >= 8192
                   else sb(st_qk, f"qTb{s_i}", [128, 8192], BF16, side="right")]
            QTb = [Buf(), Buf()]

            def em_qT_load(half):
                B.group("sync", [bf("wbf_qkv")], [QTb[half]],
                        [lambda s, j=j, half=half: s.dma_start(
                            out=qTs[half][:, 1024 * j:1024 * (j + 1)],
                            in_=wbf_qkv[1024 * half:1024 * (half + 1), 128 * j:128 * (j + 1)],
                            transpose=True) for j in range(8)], "sqT" if half == 0 else "sqT1", 16 * 8)

            xbE = [sb(st_qk, f"xbE{s_i}{k}", [128, 1024], F32, side="right") for k in range(2)]
            xb4 = [xb[0], xb[1], xbE[0], xbE[1]]
            bufs4 = ((0, "sxb0"), (1, "sxb1"), (2, "sxb2"), (3, "sxb3"))
            if s_i == 0:
                for i in range(NI):
                    em_ln_chunk(s_i, stats, STATS, i, bufs=bufs4, xbl=xb4)
                em_big_casts(STATS.w)
            if DBG and s_i == 0:
                B.group("sync", [STATS], [], [lambda s, stats=stats: s.dma_start(out=dbg_o["d_var"][:, :], in_=stats[:, :])], "ds", 16)
            vview = stats.rearrange("p (i two) -> p two i", two=2)[:, 1, :]
            B.group("scalar", [STATS, bf("cols")], [bf("rstd0")],
                    [lambda s, vview=vview: s.activation(sc[:, 32:32 + NI], vview, AF.Sqrt, bias=cols[:, CEP:CEP + 1])], "ac")
            mview = stats.rearrange("p (i two) -> p two i", two=2)[:, 0, :]
            B.group("vector", [bf("rstd0")], [STATS, bf(f"mrs{s_i % 2}")], [
                lambda v, vview=vview: v.reciprocal(vview, sc[:, 32:32 + NI]),
                lambda v, vview=vview, mview=mview, mrs=mrs: v.tensor_tensor(mrs[:, :], mview, vview, ALU.mult),
                lambda v, mrs=mrs: v.tensor_single_scalar(mrs[:, :], mrs[:, :], -1.0, ALU.mult),
            ], "vd")
            for i in range(NI):
                bi, semn = bufs4[i % 4]
                xbp = xb4[bi]
                B.group("sync", [bf(_nm(xbp))], [bf(_nm(xbp))],
                        [lambda s, s_i=s_i, i=i, xbp=xbp: s.dma_start(
                            out=xbp[:, :], in_=x_in[s_i, 128 * i:128 * (i + 1), :])], semn, 16)
                em_ln_norm(i, bufs=bufs4, xbl=xb4)
                if i == 1:
                    em_qT_load(0)
                    em_qT_load(1)
            def tail_hex(tc):
                B.group("tensor", [bf("hexT"), XNS], [PS["pC"]],
                        [lambda t, j=j, tc=tc: t.matmul(
                            pC[0:6, :], hexT[:, 6 * j:6 * (j + 1)], xnr[:, j, 512 * tc:512 * (tc + 1)],
                            start=(j == 0), stop=(j == 7)) for j in range(8)], "tp")
                B.group("scalar", [PS["pC"]], [SBT],
                        [lambda s, tc=tc: s.activation(sb_t[:, 512 * tc:512 * (tc + 1)], pC[0:6, :], AF.Tanh)], "ac")
                B.group("vector", [SBT, bf("col6")], [SBL],
                        [lambda v, tc=tc: v.tensor_scalar_mul(sbL_t[:, 512 * tc:512 * (tc + 1)],
                                                              sb_t[:, 512 * tc:512 * (tc + 1)], col6[0:6, 0:1])], "vd")

            def tail_q6(tc):
                B.group("tensor", [bf("q6T"), XNS], [PS["pD"]],
                        [lambda t, j=j, tc=tc: t.matmul(
                            pD[0:6, :], q6T[:, 6 * j:6 * (j + 1)], xnr[:, j, 512 * tc:512 * (tc + 1)],
                            start=(j == 0), stop=(j == 7)) for j in range(8)], "tp")
                B.group("scalar", [PS["pD"]], [HXC],
                        [lambda s, tc=tc: s.activation(hexc_t[:, 512 * tc:512 * (tc + 1)], pD[0:6, :], AF.Tanh)], "ac")

            def tail_asm0(tc):
                B.group("tensor", [SBT, HXC, bf("eyeT6"), bf("eyeB6")], [PS["pC"]], [
                    lambda t, tc=tc: t.matmul(pC[0:12, :], eyeT6[:, :], sb_t[:, 512 * tc:512 * (tc + 1)],
                                              start=True, stop=False),
                    lambda t, tc=tc: t.matmul(pC[0:12, :], eyeB6[:, :], hexc_t[:, 512 * tc:512 * (tc + 1)],
                                              start=False, stop=True)], "tp")
                B.group("vector", [PS["pC"]], [L0],
                        [lambda v, tc=tc: v.tensor_copy(lhs0_t[:, 512 * tc:512 * (tc + 1)], pC[0:12, :])], "vd")

            def tail_asmL(tc):
                B.group("tensor", [SBL, HXC, bf("eyeT6"), bf("eyeB6")], [PS["pD"]], [
                    lambda t, tc=tc: t.matmul(pD[0:12, :], eyeT6[:, :], sbL_t[:, 512 * tc:512 * (tc + 1)],
                                              start=True, stop=False),
                    lambda t, tc=tc: t.matmul(pD[0:12, :], eyeB6[:, :], hexc_t[:, 512 * tc:512 * (tc + 1)],
                                              start=False, stop=True)], "tp")
                B.group("vector", [PS["pD"]], [LL],
                        [lambda v, tc=tc: v.tensor_copy(lhsL_t[0:12, 512 * tc:512 * (tc + 1)], pD[0:12, :])], "vd")

            tail_q = [f for tc in range(TC) for f in (lambda tc=tc: tail_hex(tc), lambda tc=tc: tail_q6(tc))]
            tail_q += [f for tc in range(TC) for f in (lambda tc=tc: tail_asm0(tc), lambda tc=tc: tail_asmL(tc))]

            for half in range(2):
                for m in range(8):
                    for tc in range(TC):
                        pp, ppB = (pA, PS["pA"]) if (m * TC + tc) % 2 == 0 else (pB, PS["pB"])
                        B.group("tensor", [QTb[half], XNS], [ppB],
                                [lambda t, j=j, m=m, tc=tc, pp=pp, half=half: t.matmul(
                                    pp[:, :], qTs[half][:, 1024 * j + 128 * m:1024 * j + 128 * (m + 1)],
                                    xnr[:, j, 512 * tc:512 * (tc + 1)],
                                    start=(j == 0), stop=(j == 7)) for j in range(8)], "tp")
                        if half == 0:
                            B.group("scalar", [ppB, bf("cols")], [QS],
                                    [lambda s, m=m, tc=tc, pp=pp: s.activation(
                                        qr[:, m, 512 * tc:512 * (tc + 1)], pp[:, :], AF.Identity,
                                        scale=cols[:, CIB:CIB + 1])], "ac")
                        else:
                            B.group("vector", [ppB], [KS],
                                    [lambda v, m=m, tc=tc, pp=pp: v.tensor_copy(
                                        krr[:, m, 512 * tc:512 * (tc + 1)], pp[:, :])], "vd")
                            if (m * TC + tc) % 2 == 0 and tail_q:
                                tail_q.pop(0)()
            for f in tail_q:
                f()
            B.fence(("sync", "gpsimd"))
            st_qk.close()

            st_v = contextlib.ExitStack()
            vT = sb(st_v, f"vT{s_i}", [128, 8192], BF16, side="right")
            VT = Buf()
            B.group("sync", [bf("wbf_qkv")], [VT],
                    [lambda s, j=j: s.dma_start(out=vT[:, 1024 * j:1024 * (j + 1)],
                                                in_=wbf_qkv[2048:3072, 128 * j:128 * (j + 1)],
                                                transpose=True) for j in range(8)], "svT", 16 * 8)
            B.group("vector", [], [VS], [lambda v: v.memset(v_ext[:, :], 1.0)], "vd")
            for jc in range(NI):
                B.group("tensor", [VT, XNS], [PS["pE"]],
                        [lambda t, j=j, jc=jc, hh=hh: t.matmul(
                            pE_[:, 512 * hh:512 * (hh + 1)], xnr[:, j, 128 * jc:128 * (jc + 1)],
                            vT[:, 1024 * j + 512 * hh:1024 * j + 512 * (hh + 1)],
                            start=(j == 0), stop=(j == 7))
                         for hh in range(2) for j in range(8)], "tp")
                B.group("vector", [PS["pE"]], [VS],
                        [lambda v, jc=jc: v.tensor_copy(ver[:, jc, :, 0:64],
                                                        pE_.rearrange("p (h c) -> p h c", c=64))], "vd")
            if DBG and s_i == 0:
                B.group("sync", [STATS], [], [lambda s, stats=stats: s.dma_start(out=dbg_o["d_stats"][:, :], in_=stats[:, :])], "ds", 16)
                B.group("sync", [XNS], [], [lambda s: s.dma_start(out=dbg_o["d_xn"][:, :], in_=xn_slab[:, :])], "ds", 16)
                B.group("sync", [QS], [], [lambda s: s.dma_start(out=dbg_o["d_q"][:, :], in_=q_slab[:, :])], "ds", 16)
                B.group("sync", [KS], [], [lambda s: s.dma_start(out=dbg_o["d_k"][:, :], in_=k_slab[:, :])], "ds", 16)
                B.group("sync", [VS], [], [lambda s: s.dma_start(out=dbg_o["d_vext"][:, :], in_=v_ext[:, :])], "ds", 16)
                B.group("sync", [L0], [], [lambda s: s.dma_start(out=dbg_o["d_lhs0"][:, :], in_=lhs0_t[:, :])], "ds", 16)
                B.group("sync", [LL], [], [lambda s: s.dma_start(out=dbg_o["d_lhsL"][:, :], in_=lhsL_t[:, :])], "ds", 16)
            B.fence(("sync", "gpsimd"))
            st_v.close()
            B.fence(("sync", "gpsimd"))
            st_ln.close()

            st_fl = contextlib.ExitStack()
            crh = [sb(st_fl, f"crh{s_i}{p}", [128, T], BF16, side="right") for p in range(2)]
            t_sb = [sb(st_fl, f"t_sb{s_i}{p}", [128, 512], BF16, side="right") for p in range(2)]
            s_sb = [sb(st_fl, f"s_sb{s_i}0", [128, 2048], F32, side="right")]
            P_sb = [sb(st_fl, f"P_sb{s_i}{p}", [128, 2048], BF16, side="right") for p in range(3)]
            rb128 = sb(st_fl, f"rb128{s_i}", [128, 512], F32, side="right")
            tmp_l = [rb128[0:1, :], rb128[32:33, :]]
            st_out = contextlib.ExitStack()
            attnout = sb(st_out, f"attnout{s_i}", [128, 8 * T], BF16, side="left")
            ATT = Buf()
            B.group("vector", [], [LSL], [lambda v: v.memset(l_slab[:, :], 1.0)], "vd")
            B.group("vector", [], [CRHb[0]], [lambda v: v.memset(crh[0][:, :], 0.0)], "vd")
            B.group("vector", [], [CRHb[1]], [lambda v: v.memset(crh[1][:, :], 0.0)], "vd")
            ar = attnout.rearrange("p (m t) -> p m t", t=T)
            tasks = []
            for h in range(H):
                for ic in range(NICH):
                    njj = 4 * ic + 4
                    for jj in range(njj):
                        tasks.append((h, ic, jj, jj == 0, jj == njj - 1))
            NT_ = len(tasks)
            for _k2 in ("pE0", "pE1"):
                bf(_k2).w = PS["pE"].w
                bf(_k2).readers = list(PS["pE"].readers)
            CBK = [(pC, PS["pC"]), (pD, PS["pD"]), (pE_, bf("pE0"))]
            AVB = [(pO, PS["pO"], 0), (pE_, bf("pE1"), 512)]

            def em_crh(h, tcs=None):
                for tc in (range(TC) if tcs is None else tcs):
                    B.group("tensor", [L0, bf("TTall")], [PS["pC"]],
                            [lambda t, h=h, tc=tc: t.matmul(pC[0:12, :], TTall[:, 12 * h:12 * (h + 1)],
                                                            lhs0_t[:, 512 * tc:512 * (tc + 1)],
                                                            start=True, stop=True)], "tp")
                    B.group("vector", [PS["pC"]], [CRHb[h % 2]],
                            [lambda v, h=h, tc=tc: v.tensor_copy(crh[h % 2][0:12, 512 * tc:512 * (tc + 1)],
                                                                 pC[0:12, :])], "vd")

            def em_raw(idx):
                h, ic, jj, _, _ = tasks[idx]
                p = idx % 2
                hp, rbo = h // 2, 64 * (h % 2)
                pR, pRB = (pA, PS["pA"]) if p == 0 else (pB, PS["pB"])
                B.group("tensor", [QS, KS], [pRB],
                        [lambda t, hp=hp, rbo=rbo, jj=jj, ic=ic, pR=pR: t.matmul(
                            pR[:, :], krr[rbo:rbo + 64, hp, 128 * jj:128 * (jj + 1)],
                            qr[rbo:rbo + 64, hp, 512 * ic:512 * (ic + 1)],
                            start=True, stop=True)], "tp")

            def em_combo(idx):
                h, ic, jj, _, _ = tasks[idx]
                pCc, pCB = CBK[idx % 3]
                B.group("tensor", [LL, CRHb[h % 2]], [pCB],
                        [lambda t, h=h, jj=jj, ic=ic, pCc=pCc: t.matmul(
                            pCc[:, 0:512], lhsL_t[:, 128 * jj:128 * (jj + 1)],
                            crh[h % 2][:, 512 * ic:512 * (ic + 1)], start=True, stop=True)], "tp")

            def em_tanh(idx):
                p = idx % 2
                pR = pA if p == 0 else pB
                pRB = PS["pA"] if p == 0 else PS["pB"]
                B.group("scalar", [pRB], [TSBb[p]],
                        [lambda s, p=p, pR=pR: s.activation(t_sb[p][:, :], pR[:, :], AF.Tanh)], "ac")

            def em_combine(idx):
                p = idx % 2
                pCc, pCB = CBK[idx % 3]
                qd = 0
                col = 512 * (idx % 4)
                B.group("vector", [TSBb[p], pCB, bf("cols")], [SSBb[qd]],
                        [lambda v, p=p, pCc=pCc, qd=qd, col=col: v.scalar_tensor_tensor(
                            s_sb[qd][:, col:col + 512], t_sb[p][:, :], cols[:, CAL:CAL + 1],
                            pCc[:, 0:512], op0=ALU.mult, op1=ALU.add)], "vd")

            def em_exp(q, nsub):
                qd = 0
                pq = q % 3
                B.group("scalar", [SSBb[qd], bf("cols")], [PSBb[pq]],
                        [lambda s, qd=qd, pq=pq, nsub=nsub: s.activation(
                            P_sb[pq][:, 0:512 * nsub], s_sb[qd][:, 0:512 * nsub], AF.Exp,
                            bias=cols[:, CNM:CNM + 1])], "ac")

            def em_mask(idx):
                h, ic, jj, _, _ = tasks[idx]
                if jj < 4 * ic:
                    return
                c = jj - 4 * ic
                pq = (idx // 4) % 3
                col = 512 * (idx % 4)
                B.group("vector", [PSBb[pq], bf("maskW")], [PSBb[pq]],
                        [lambda v, pq=pq, col=col, c=c: v.tensor_tensor(
                            P_sb[pq][:, col:col + 512], P_sb[pq][:, col:col + 512],
                            maskW[:, 384 - 128 * c:896 - 128 * c], ALU.mult)], "vd", 1)

            gctr = [0]
            def em_av(idx):
                h, ic, jj, first, last = tasks[idx]
                pq = (idx // 4) % 3
                col = 512 * (idx % 4)
                gp_ = gctr[0] % 2
                pv, pvB, off = AVB[gp_]
                B.group("tensor", [PSBb[pq], VS], [pvB],
                        [lambda t, h=h, jj=jj, pq=pq, col=col, first=first, last=last, pv=pv, off=off: t.matmul(
                            pv[0:65, off:off + 512], ver[:, jj, h, :], P_sb[pq][:, col:col + 512],
                            start=first, stop=last)], "tp")
                if last:
                    hp, rbo = h // 2, 64 * (h % 2)
                    B.group("vector", [pvB], [ATT, TMPL[gp_]], [
                        lambda v, hp=hp, rbo=rbo, ic=ic, pv=pv, off=off: v.tensor_copy(
                            ar[rbo:rbo + 64, hp, 512 * ic:512 * (ic + 1)], pv[0:64, off:off + 512]),
                        lambda v, pv=pv, off=off, gp_=gp_: v.tensor_copy(tmp_l[gp_][0:1, :], pv[64:65, off:off + 512]),
                    ], "vd")
                    B.group("sync", [TMPL[gp_]], [LSL],
                            [lambda s, h=h, ic=ic, gp_=gp_: s.dma_start(
                                out=l_slab[NICH * h + ic:NICH * h + ic + 1, :], in_=tmp_l[gp_][0:1, :])],
                            f"stl{gp_}", 16)
                    gctr[0] += 1

            em_crh(0)
            if H > 1:
                em_crh(1)
            crh_q = []
            for idx in range(NT_ + 8):
                if idx < NT_:
                    h, ic, jj, first, last = tasks[idx]
                    if first and ic == 0 and h + 2 < H:
                        crh_q.extend((h + 2, tc) for tc in range(TC))
                    em_raw(idx)
                    em_tanh(idx)
                    em_combo(idx)
                if s_i + 1 < SEQ and idx % 38 == 19 and idx // 38 < NI:
                    em_ln_chunk(s_i + 1, stats2[(s_i + 1) % 2], bf(f"stats{(s_i + 1) % 2}"), idx // 38)
                j1 = idx - 1
                if 0 <= j1 < NT_:
                    em_combine(j1)
                    if j1 % 3 == 1 and crh_q:
                        hh2, tc2 = crh_q.pop(0)
                        em_crh(hh2, [tc2])
                    if j1 % 4 == 3:
                        em_exp(j1 // 4, 4)
                    elif j1 == NT_ - 1:
                        em_exp(j1 // 4, (j1 % 4) + 1)
                j4m = idx - 4
                if 0 <= j4m < NT_:
                    em_mask(j4m)
                j6 = idx - 8
                if 0 <= j6 < NT_:
                    em_av(j6)
            B.wait_tok("vector", ("stl0", B.sems["stl0"][1], "sync"))
            B.wait_tok("vector", ("stl1", B.sems["stl1"][1], "sync"))
            B.group("vector", [LSL], [bf("rinv"), bf("rhl")], [
                lambda v: v.reciprocal(rinv_s[:, :], l_slab[:, :]),
                lambda v: v.tensor_copy(rhl[:, 0:512], rinv_s[:, :]),
                lambda v: v.tensor_tensor(rhl[:, 512:1024], rinv_s[:, :], rhl[:, 0:512], ALU.subtract),
            ], "vd")
            for h in range(H):
                hp, rbo = h // 2, 64 * (h % 2)
                for ic in range(NICH):
                    r = NICH * h + ic
                    B.group("tensor", [bf("rhl"), bf("eye64")], [PS["pE"]], [
                        lambda t, r=r: t.matmul(pE_[0:64, 0:512], eye64[:, r:r + 1].broadcast_to([64, 64]),
                                                rhl[:, 0:512], start=True, stop=False),
                        lambda t, r=r: t.matmul(pE_[0:64, 0:512], eye64[:, r:r + 1].broadcast_to([64, 64]),
                                                rhl[:, 512:1024], start=False, stop=True)], "tp")
                    B.group("vector", [PS["pE"]], [RB128],
                            [lambda v, rbo=rbo: v.tensor_copy(rb128[rbo:rbo + 64, :], pE_[0:64, 0:512])], "vd")
                    B.group("vector", [RB128], [ATT],
                            [lambda v, hp=hp, rbo=rbo, ic=ic: v.tensor_tensor(
                                ar[rbo:rbo + 64, hp, 512 * ic:512 * (ic + 1)],
                                ar[rbo:rbo + 64, hp, 512 * ic:512 * (ic + 1)],
                                rb128[rbo:rbo + 64, :], ALU.mult)], "vd")
            if DBG and s_i == 0:
                B.group("sync", [ATT], [], [lambda s: s.dma_start(out=dbg_o["d_att"][:, :], in_=attnout[:, :])], "ds", 16)
                B.group("sync", [CRHb[1]], [], [lambda s: s.dma_start(out=dbg_o["d_crh0"][:, :], in_=crh[1][:, :])], "ds", 16)
                B.group("sync", [bf("rinv")], [], [lambda s: s.dma_start(out=dbg_o["d_lsl"][:, :], in_=l_slab[:, :])], "ds", 16)
            B.fence()
            st_fl.close()
            st_seq.close()

            st_ffn = contextlib.ExitStack()
            ffnout = sb(st_out, f"ffnout{s_i}", [128, 8 * T], BF16, side="left")
            FFO = Buf()
            fr = ffnout.rearrange("p (m t) -> p m t", t=T)
            wp = [sb(st_ffn, f"wp{s_i}{p}", [128, 8192], BF16, side="right") for p in range(2)]
            WP = [Buf(), Buf()]
            vwuS = sb(st_ffn, f"vwuS{s_i}", [128, 3 * 4096], BF16, side="right")
            VWU = [Buf(), Buf(), Buf()]
            xn_c = sb(st_ffn, f"xnc{s_i}", [128, 4096], BF16, side="right")
            qc_s = sb(st_ffn, f"qc{s_i}", [128, 4096], BF16, side="right")
            kc_s = sb(st_ffn, f"kc{s_i}", [128, 4096], BF16, side="right")
            iqc_s = sb(st_ffn, f"iqc{s_i}", [128, 4096], BF16, side="right")
            ikc_s = sb(st_ffn, f"ikc{s_i}", [128, 4096], BF16, side="right")
            cum_s = sb(st_ffn, f"cum{s_i}", [128, 512], F32, side="right")
            ptmp = sb(st_ffn, f"ptmp{s_i}", [128, 512], BF16, side="right")
            prod_s = sb(st_ffn, f"prod{s_i}", [128, 2048], BF16, side="right")
            hbufs = [sb(st_ffn, f"hb{s_i}0", [128, 512], BF16, side="right"), ptmp]
            lg_sb = sb(st_ffn, f"lg{s_i}", [8, 512], F32, side="right")
            wfm = sb(st_ffn, f"wfm{s_i}", [8, 1024], BF16, side="right")
            prefix = sb(st_ffn, f"prefix{s_i}", [128, 8], F32, side="right")
            wtm = sb(st_ffn, f"wtm{s_i}", [128, 32], BF16, side="right")
            XNC, QC, KC, IQC, IKC = Buf(), Buf(), Buf(), Buf(), Buf()
            CUM, PTMP, PROD, LG, PFX, WTM = Buf(), Buf(), Buf(), Buf(), Buf(), Buf()
            WFMb = [Buf(), Buf()]
            HBb = [Buf(), Buf()]
            B.group("vector", [], [PFX], [lambda v: v.memset(prefix[:, :], 0.0)], "vd")

            def load_wp(p, src_dram):
                B.group("sync", [bf(_nm(src_dram))], [WP[p]],
                        [lambda s, j=j, p=p, src_dram=src_dram: s.dma_start(
                            out=wp[p][:, 1024 * j:1024 * (j + 1)],
                            in_=src_dram[:, 128 * j:128 * (j + 1)], transpose=True)
                         for j in range(8)], f"swp{p}", 16 * 8)

            def proj(wbuf_p, rhs_slab, rhs_buf, dst_r, dst_buf, evac_eng):
                for m in range(8):
                    pp, ppB = (pA, PS["pA"]) if m % 2 == 0 else (pB, PS["pB"])
                    B.group("tensor", [WP[wbuf_p], rhs_buf], [ppB],
                            [lambda t, j=j, m=m, pp=pp, wbuf_p=wbuf_p, rhs_slab=rhs_slab: t.matmul(
                                pp[:, :], wp[wbuf_p][:, 1024 * j + 128 * m:1024 * j + 128 * (m + 1)],
                                rhs_slab.rearrange("p (d t) -> p d t", t=512)[:, j, :],
                                start=(j == 0), stop=(j == 7)) for j in range(8)], "tp")
                    if evac_eng == "vector":
                        B.group("vector", [ppB], [dst_buf],
                                [lambda v, m=m, pp=pp, dst_r=dst_r: v.tensor_copy(
                                    dst_r.rearrange("p (d t) -> p d t", t=512)[:, m, :], pp[:, :])], "vd")
                    else:
                        B.group("scalar", [ppB], [dst_buf],
                                [lambda s, m=m, pp=pp, dst_r=dst_r: s.copy(
                                    dst_r.rearrange("p (d t) -> p d t", t=512)[:, m, :], pp[:, :])], "ac")

            def prewarm(func):
                B.group("scalar", [], [],
                        [lambda s, func=func: s.activation(sc[0:1, 62:63], sc[0:1, 63:64], func)], "ac")

            def em_rebuild_dma(tc, ss):
                i = 4 * tc + ss
                p = ss % 2
                B.group("sync", [XBb[p]], [XBb[p]],
                        [lambda s, s_i=s_i, i=i, p=p: s.dma_start(
                            out=xb[p][:, :], in_=x_in[s_i, 128 * i:128 * (i + 1), :])], f"sxb{p}", 16)

            def em_rebuild_id(tc, ss):
                i = 4 * tc + ss
                p = ss % 2
                B.group("scalar", [XBb[p], STATS, bf(f"mrs{s_i % 2}")], [XT],
                        [lambda s2, i=i, p=p, stats=stats, mrs=mrs: s2.activation(
                            xt_tm[:, :], xb[p][:, :], AF.Identity,
                            bias=mrs[:, i:i + 1], scale=stats[:, 2 * i + 1:2 * i + 2])], "ac")

            def em_rebuild_tp(tc, ss):
                B.group("tensor", [XT, bf("eye")], [PS["pTb"]],
                        [lambda t, d=d: t.transpose(pTb[:, 128 * d:128 * (d + 1)],
                                                    xt_tm[:, 128 * d:128 * (d + 1)], eye[:, :])
                         for d in range(8)], "tp")
                B.group("vector", [PS["pTb"]], [XNC],
                        [lambda v, ss=ss: v.tensor_copy(
                            xn_c.rearrange("p (d t) -> p d t", t=512)[:, :, 128 * ss:128 * (ss + 1)],
                            pTb.rearrange("p (d t) -> p d t", t=128))], "vd")

            for ss in range(4):
                em_rebuild_dma(0, ss)
                em_rebuild_id(0, ss)
                em_rebuild_tp(0, ss)
            for tc in range(TC):
                load_wp(0, wbf_wq)
                load_wp(1, wbf_wk)
                proj(0, xn_c, XNC, qc_s, QC, "vector")
                load_wp(0, wbf_winv)
                proj(1, xn_c, XNC, kc_s, KC, "scalar")
                load_wp(1, wbf_rw1)
                proj(0, qc_s, QC, iqc_s, IQC, "vector")
                prewarm(AF.Gelu)
                proj(0, kc_s, KC, ikc_s, IKC, "scalar")
                xtv = xt_tm.rearrange("p (a c) -> p a c", a=8)

                def em_router_pre(br, hook_m=None, hook=None):
                    src_slab, src_buf = (qc_s, QC) if br == 0 else (iqc_s, IQC)

                    def em_logits(m):
                        hb = m % 2
                        B.group("tensor", [HBb[hb], bf("rw2T")], [PS["pD"]],
                                [lambda t, m=m, hb=hb: t.matmul(pD[0:8, :], rw2T[:, 8 * m:8 * (m + 1)],
                                                                hbufs[hb][:, :], start=(m == 0), stop=(m == 7))], "tp")

                    for m in range(8):
                        hb = m % 2
                        pp, ppB = (pC, PS["pC"]) if hb == 0 else (pA, PS["pA"])
                        B.group("tensor", [WP[1], src_buf], [ppB],
                                [lambda t, j=j, m=m, pp=pp, src_slab=src_slab: t.matmul(
                                    pp[:, :], wp[1][:, 1024 * j + 128 * m:1024 * j + 128 * (m + 1)],
                                    src_slab.rearrange("p (d t) -> p d t", t=512)[:, j, :],
                                    start=(j == 0), stop=(j == 7)) for j in range(8)], "tp")
                        B.group("scalar", [ppB, bf("b1c")], [HBb[hb]],
                                [lambda s, m=m, hb=hb, pp=pp: s.activation(hbufs[hb][:, :], pp[:, :], AF.Gelu,
                                                                           bias=b1c[:, m:m + 1])], "ac")
                        if m > 0:
                            em_logits(m - 1)
                        if hook is not None and m == hook_m:
                            hook()
                    em_logits(7)
                    prewarm(AF.Exp)

                def em_router_fwdT(br):
                    mx = 56 if br == 0 else 40
                    B.group("vector", [PS["pD"], bf("b2eb")], [LG],
                            [lambda v: v.tensor_single_scalar(lg_sb[:, :], pD[0:8, :], b2eb[:, 0:1], ALU.add)], "vd")
                    B.group("tensor", [LG, bf("eyef")], [PS["pE"]],
                            [lambda t, ss=ss: t.transpose(pE_[:, 8 * ss:8 * (ss + 1)],
                                                          lg_sb[:, 128 * ss:128 * (ss + 1)],
                                                          eyef[0:8, 0:8]) for ss in range(4)], "tp")
                    B.group("vector", [PS["pE"]], [bf("smx")],
                            [lambda v, ss=ss, mx=mx: v.reduce_max(sc[:, mx + ss:mx + ss + 1],
                                                                  pE_[:, 8 * ss:8 * (ss + 1)],
                                                                  axis=AX.X, negate=True) for ss in range(4)], "vd")
                    for ss in range(4):
                        B.group("scalar", [PS["pE"], bf("smx")], [XT],
                                [lambda s, ss=ss, br=br, mx=mx: s.activation(
                                    xtv[:, 4 * br + ss, 0:8],
                                    pE_[:, 8 * ss:8 * (ss + 1)], AF.Exp, bias=sc[:, mx + ss:mx + ss + 1])], "ac")
                    if br == 0:
                        prewarm(AF.Gelu)

                def em_router_backT(br, tc):
                    sm, rc = (44, 48) if br == 0 else (32, 36)
                    emits = [lambda v, ss=ss, br=br, sm=sm: v.reduce_sum(
                        sc[:, sm + ss:sm + ss + 1], xtv[:, 4 * br + ss, 0:8], axis=AX.X) for ss in range(4)]
                    emits.append(lambda v, sm=sm, rc=rc: v.reciprocal(sc[:, rc:rc + 4], sc[:, sm:sm + 4]))
                    emits += [lambda v, ss=ss, br=br, tc=tc, rc=rc: v.tensor_scalar(
                        wtm[:, 8 * ss:8 * (ss + 1)], xtv[:, 4 * br + ss, 0:8],
                        sc[:, rc + ss:rc + ss + 1], invn[:, 4 * tc + ss:4 * tc + ss + 1],
                        op0=ALU.mult, op1=ALU.mult) for ss in range(4)]
                    B.group("vector", [XT, bf("invn")], [WTM], emits, "vd")
                    B.group("tensor", [WTM, bf("eye")], [PS["pTb"]],
                            [lambda t, ss=ss: t.transpose(pTb[0:8, 128 * ss:128 * (ss + 1)],
                                                          wtm[:, 8 * ss:8 * (ss + 1)], eye[:, :])
                             for ss in range(4)], "tp")
                    B.group("vector", [PS["pTb"]], [WFMb[br]],
                            [lambda v, br=br: v.tensor_copy(wfm[:, 512 * br:512 * (br + 1)], pTb[0:8, 0:512])], "vd")

                em_router_pre(0)
                em_router_fwdT(0)
                em_router_pre(1, hook_m=3, hook=lambda tc=tc: em_router_backT(0, tc))
                em_router_fwdT(1)
                for br in range(2):
                    vsrc = dr["vbf_f"] if br == 0 else dr["vbf_i"]
                    wsrc = dr["wbfr_f"] if br == 0 else dr["wbfr_i"]
                    usrc = dr["ubf_f"] if br == 0 else dr["ubf_i"]
                    bmat, bmat_b = (bfwd_s, bf("bfwd_s")) if br == 0 else (binv_s, bf("binv_s"))
                    qsrc, qbuf = (qc_s, QC) if br == 0 else (iqc_s, IQC)
                    ksrc, kbuf = (kc_s, KC) if br == 0 else (ikc_s, IKC)
                    B.group("sync", [bf(_nm(vsrc))], [VWU[0]],
                            [lambda s, vsrc=vsrc: s.dma_start(
                                out=vwuS[:, 0:4096].rearrange("p (j c) -> p j c", j=8),
                                in_=vsrc.rearrange("(j p) c -> p j c", p=128))], "svS", 16)
                    B.group("sync", [bf(_nm(wsrc))], [VWU[1]],
                            [lambda s, wsrc=wsrc: s.dma_start(
                                out=vwuS[:, 4096:8192].rearrange("p (j c) -> p j c", j=8),
                                in_=wsrc.rearrange("(j p) c -> p j c", p=128))], "swS", 16)
                    B.group("sync", [bf(_nm(usrc))], [VWU[2]],
                            [lambda s, c=c, usrc=usrc: s.dma_start(
                                out=vwuS[:, 8192 + 1024 * c:8192 + 1024 * (c + 1)],
                                in_=usrc[:, 128 * c:128 * (c + 1)], transpose=True) for c in range(4)],
                            "suS", 16 * 4)
                    for c in range(4):
                        B.group("tensor", [VWU[1], kbuf], [PS["pB"]],
                                [lambda t, j=j, c=c, ksrc=ksrc: t.matmul(
                                    pB[:, :], vwuS[:, 4096 + 512 * j + 128 * c:4096 + 512 * j + 128 * (c + 1)],
                                    ksrc.rearrange("p (d t) -> p d t", t=512)[:, j, :],
                                    start=(j == 0), stop=(j == 7)) for j in range(8)], "tp")
                        if br == 1 and c == 0 and tc + 1 < TC:
                            em_rebuild_tp(tc + 1, 3)
                        B.group("tensor", [VWU[0], qbuf], [PS["pA"]],
                                [lambda t, j=j, c=c, qsrc=qsrc: t.matmul(
                                    pA[:, :], vwuS[:, 512 * j + 128 * c:512 * j + 128 * (c + 1)],
                                    qsrc.rearrange("p (d t) -> p d t", t=512)[:, j, :],
                                    start=(j == 0), stop=(j == 7)) for j in range(8)], "tp")
                        if br == 0 and c == 1:
                            em_router_backT(1, tc)
                        B.group("tensor", [WFMb[br], bf("selE")], [PS["pC"]],
                                [lambda t, c=c, br=br: t.matmul(pC[:, :], selE[:, 128 * c:128 * (c + 1)],
                                                                wfm[:, 512 * br:512 * (br + 1)],
                                                                start=True, stop=True)], "tp")
                        B.group("vector", [PS["pB"], PFX, bf("maskW")], [CUM],
                                [lambda v, br=br, c=c: v.tensor_tensor_scan(
                                    cum_s[:, :], pB[:, :], maskW[:, 0:512],
                                    prefix[:, 4 * br + c:4 * br + c + 1], op0=ALU.add, op1=ALU.bypass)], "vd")
                        B.group("vector", [CUM], [PFX],
                                [lambda v, br=br, c=c: v.tensor_copy(prefix[:, 4 * br + c:4 * br + c + 1],
                                                                     cum_s[:, 511:512])], "vd")
                        B.group("vector", [PS["pA"], CUM], [PTMP],
                                [lambda v: v.tensor_tensor(ptmp[:, :], pA[:, :], cum_s[:, :], ALU.mult)], "vd")
                        B.group("vector", [PTMP, PS["pC"]], [PROD],
                                [lambda v, c=c: v.tensor_tensor(prod_s[:, 512 * c:512 * (c + 1)],
                                                                ptmp[:, :], pC[:, :], ALU.mult)], "vd")
                        if br == 0 and tc + 1 < TC and c < 3:
                            if c < 2:
                                em_rebuild_dma(tc + 1, c)
                            else:
                                em_rebuild_id(tc + 1, 0)
                                em_rebuild_dma(tc + 1, 2)
                    for m in range(8):
                        if br == 0 and tc + 1 < TC:
                            if m in (0, 3, 6):
                                em_rebuild_tp(tc + 1, m // 3)
                                em_rebuild_id(tc + 1, m // 3 + 1)
                                if m == 0:
                                    em_rebuild_dma(tc + 1, 3)
                        pU, pUB = [(pD, PS["pD"]), (pC, PS["pC"]), (pO, PS["pO"])][m % 3]
                        emits = [lambda t, c=c, m=m, pU=pU: t.matmul(
                            pU[:, :], vwuS[:, 8192 + 1024 * c + 128 * m:8192 + 1024 * c + 128 * (m + 1)],
                            prod_s[:, 512 * c:512 * (c + 1)], start=(c == 0), stop=False) for c in range(4)]
                        emits.append(lambda t, m=m, br=br, bmat=bmat, pU=pU: t.matmul(
                            pU[:, :], bmat[:, 128 * m:128 * (m + 1)], wfm[:, 512 * br:512 * (br + 1)],
                            start=False, stop=True))
                        B.group("tensor", [VWU[2], PROD, WFMb[br], bmat_b], [pUB], emits, "tp")
                        if br == 0:
                            B.group("scalar", [pUB], [FFO],
                                    [lambda s2, m=m, tc=tc, pU=pU: s2.copy(
                                        fr[:, m, 512 * tc:512 * (tc + 1)], pU[:, :])], "ac")
                        else:
                            B.group("vector", [pUB, bf("cols")], [FFO],
                                    [lambda v, m=m, tc=tc, pU=pU: v.scalar_tensor_tensor(
                                        fr[:, m, 512 * tc:512 * (tc + 1)], pU[:, :],
                                        cols[:, CBI:CBI + 1], fr[:, m, 512 * tc:512 * (tc + 1)],
                                        op0=ALU.mult, op1=ALU.add)], "vd")
            if DBG and s_i == 0:
                B.group("sync", [FFO], [], [lambda s: s.dma_start(out=dbg_o["d_ffn"][:, :], in_=ffnout[:, :])], "ds", 16)
            owT, woT = wp[0], wp[1]
            OWT, WOT = WP[0], WP[1]
            B.group("sync", [bf("wbf_outw")], [OWT],
                    [lambda s, j=j: s.dma_start(out=owT[:, 1024 * j:1024 * (j + 1)],
                                                in_=wbf_outw[:, 128 * j:128 * (j + 1)], transpose=True)
                     for j in range(8)], "sow", 16 * 8)
            B.group("sync", [bf("wbf_wo")], [WOT],
                    [lambda s, j=j: s.dma_start(out=woT[:, 1024 * j:1024 * (j + 1)],
                                                in_=wbf_wo[:, 128 * j:128 * (j + 1)], transpose=True)
                     for j in range(8)], "swo", 16 * 8)

            st_fin = contextlib.ExitStack()
            out_f = [xsq, sb(st_fin, f"of{s_i}1", [128, 1024], F32, side="right")]
            OFb = [bf("xsq"), Buf()]
            for jc in range(NI):
                p = jc % 2
                B.group("sync", [XBb[p]], [XBb[p]],
                        [lambda s, s_i=s_i, jc=jc, p=p: s.dma_start(
                            out=xb[p][:, :], in_=x_in[s_i, 128 * jc:128 * (jc + 1), :])], f"sxb{p}", 16)
                if p == 0:
                    ptgt = [(pE_, PS["pE"], 0), (pE_, PS["pE"], 512)]
                else:
                    ptgt = [(pD, PS["pD"], 0), (pO, PS["pO"], 0)]
                for hh in range(2):
                    pt, ptB, off = ptgt[hh]
                    emits = []
                    for dd in range(8):
                        emits.append(lambda t, dd=dd, jc=jc, hh=hh, pt=pt, off=off: t.matmul(
                            pt[:, off:off + 512], ar[:, dd, 128 * jc:128 * (jc + 1)],
                            owT[:, 1024 * dd + 512 * hh:1024 * dd + 512 * (hh + 1)],
                            start=(dd == 0), stop=False))
                    for dd in range(8):
                        emits.append(lambda t, dd=dd, jc=jc, hh=hh, pt=pt, off=off: t.matmul(
                            pt[:, off:off + 512], fr[:, dd, 128 * jc:128 * (jc + 1)],
                            woT[:, 1024 * dd + 512 * hh:1024 * dd + 512 * (hh + 1)],
                            start=False, stop=(dd == 7)))
                    B.group("tensor", [ATT, FFO, OWT, WOT], [ptB], emits, "tp")
                for hh in range(2):
                    pt, ptB, off = ptgt[hh]
                    B.group("vector", [ptB, XBb[p]], [OFb[p]],
                            [lambda v, p=p, hh=hh, pt=pt, off=off: v.tensor_tensor(
                                out_f[p][:, 512 * hh:512 * (hh + 1)], pt[:, off:off + 512],
                                xb[p][:, 512 * hh:512 * (hh + 1)], ALU.add)], "vd")
                B.group("sync", [OFb[p]], [bf("yout")],
                        [lambda s, s_i=s_i, jc=jc, p=p: s.dma_start(
                            out=y_out[s_i, 128 * jc:128 * (jc + 1), :], in_=out_f[p][:, :])], f"sst{p}", 16)
            B.fence()
            st_fin.close()
            st_ffn.close()
            st_out.close()
            B.fence(force=True)

        for s_nm, rec in sems.items():
            if s_nm in ("tp", "ac", "vd") or rec[1] == 0:
                continue
            B.emit("sync", lambda s, h=rec[0], v=rec[1]: s.wait_ge(h, v))
        B.emit("gpsimd", lambda g, h=sems["dg"][0], v=sems["dg"][1]: g.wait_ge(h, v))

        blk = E(nc.Block())
        blk.tensor(lambda e: [f(e) for f in B.ops["tensor"]])
        blk.scalar(lambda e: [f(e) for f in B.ops["scalar"]])
        blk.vector(lambda e: [f(e) for f in B.ops["vector"]])
        blk.sync(lambda e: [f(e) for f in B.ops["sync"]])
        blk.gpsimd(lambda e: [f(e) for f in B.ops["gpsimd"]])
    return nc


def host_consts(T=2048):
    bft = ml_dtypes.bfloat16
    c = {}
    c["c_eye"] = np.eye(128, dtype=bft)
    c["c_eyef"] = np.eye(128, dtype=np.float32)
    c["c_negI"] = (np.eye(128) * -30000.0).astype(bft)
    j = np.arange(128)[:, None]
    u = np.arange(896)[None, :]
    c["c_maskW"] = (u >= j + 384).astype(bft)
    c["c_eye64"] = np.eye(64, dtype=bft)
    e6 = np.zeros((6, 12), dtype=bft); e6[np.arange(6), np.arange(6)] = 1
    c["c_eyeT6"] = e6
    e6b = np.zeros((6, 12), dtype=bft); e6b[np.arange(6), 6 + np.arange(6)] = 1
    c["c_eyeB6"] = e6b
    sel = np.zeros((8, 512), dtype=bft)
    for cc in range(4):
        for m in range(128):
            sel[2 * cc + m // 64, 128 * cc + m] = 1
    c["c_selE"] = sel
    tte = np.zeros((12, 12 * 16), dtype=bft)
    for h in range(16):
        for dd in range(6):
            tte[dd, 12 * h + dd] = 1
    c["c_TTeye"] = tte
    c["c_ones1"] = np.ones((1, 128), dtype=np.float32)
    t_idx = np.arange(T).reshape(T // 128, 128).T
    c["c_invn"] = (1.0 / (t_idx + 1)).astype(np.float32)
    return c


def make_in_maps(x, weights, T=2048, SEQ=2, cores=NCORES):
    shared = dict(weights)
    shared.update(host_consts(T=T))
    in_maps = []
    for core in range(cores):
        m = dict(shared)
        m["x"] = np.ascontiguousarray(x[SEQ * core:SEQ * (core + 1), :T])
        in_maps.append(m)
    return in_maps


def pack_weights(kw):
    f = lambda a: np.ascontiguousarray(np.asarray(a, dtype=np.float32))
    return dict(
        qkv_w=f(kw["L0_attn_qkv_w"]), out_w=f(kw["L0_attn_out_w"]), hex_w=f(kw["L0_attn_hex_w"]),
        lam_logit=f(kw["L0_attn_hamming_lambda_logit"]).reshape(1), q6_w=f(kw["L0_attn_q6_w"]),
        transforms=f(kw["L0_attn_transforms"]), tw=f(kw["L0_attn_transform_weights"]),
        scale_logit=f(kw["L0_attn_scale_logit"]).reshape(1),
        sips_alpha=f(kw["L0_attn_sips_alpha"]).reshape(1),
        sips_beta=f(kw["L0_attn_sips_beta"]).reshape(1),
        w_q=f(kw["L0_ffn_W_Q"]), w_k=f(kw["L0_ffn_W_K"]), w_o=f(kw["L0_ffn_W_O"]),
        w_inv=f(kw["L0_ffn_W_inv"]), v_fwd=f(kw["L0_ffn_V_fwd"]), w_fwd=f(kw["L0_ffn_W_fwd"]),
        u_fwd=f(kw["L0_ffn_U_fwd"]), b_fwd=f(kw["L0_ffn_b_fwd"]), v_inv=f(kw["L0_ffn_V_inv"]),
        w_inv_exp=f(kw["L0_ffn_W_inv_exp"]), u_inv=f(kw["L0_ffn_U_inv"]), b_inv=f(kw["L0_ffn_b_inv"]),
        rw1=f(kw["L0_ffn_router_w1"]), rb1=f(kw["L0_ffn_router_b1"]), rw2=f(kw["L0_ffn_router_w2"]),
        rb2=f(kw["L0_ffn_router_b2"]), alpha_bi=f(kw["L0_ffn_alpha_bi"]).reshape(1),
        expert_bias=f(kw["L0_ffn_expert_bias"]),
    )


def kernel(**kw):
    T, SEQ = 2048, 2
    nc = build_program(T=T, SEQ=SEQ)
    x = np.ascontiguousarray(np.asarray(kw["L0_attn_x"], dtype=np.float32))
    in_maps = make_in_maps(x, pack_weights(kw), T=T, SEQ=SEQ)
    res = run_bass_kernel_spmd(nc, in_maps, core_ids=list(range(NCORES)))
    out = np.empty((NCORES * SEQ, T, D), dtype=np.float32)
    for core in range(NCORES):
        out[SEQ * core:SEQ * (core + 1)] = np.asarray(res.results[core]["y"])
    return out
